# version 33
# baseline (speedup 1.0000x reference)
import numpy as np
from contextlib import ExitStack

import concourse.bass as bass
import concourse.mybir as mybir
from concourse.bass_utils import run_bass_kernel_spmd

F32 = mybir.dt.float32
BF16 = mybir.dt.bfloat16
AF = mybir.ActivationFunctionType
ALU = mybir.AluOpType
AX = mybir.AxisListType.X

B, S, D = 4, 4096, 1024
NJ = 8
NTF = 32
NTO = 16
NEXP = 8
SCW = 2048
MASKW = 256
EPS = 1e-5
NEG = -30000.0
KV_RING = 8
KV_CHUNK = 4
KB = 1024


class Prog:
    ENGINES = ("tensor", "vector", "scalar", "gpsimd", "sync")

    def __init__(self, nc, ctx):
        self.nc = nc
        self.q = {e: [] for e in self.ENGINES}
        self.sem = {}
        self.cnt = {}
        for name in ("dS", "dQ", "dW", "PE", "VE", "AC", "PO"):
            self.sem[name] = ctx.enter_context(nc.semaphore(name))
            self.cnt[name] = 0

    def emit(self, eng, fn, waits=(), inc=None, n=1):
        waits = tuple(w for w in waits if w is not None)
        tok = None
        if inc is not None:
            self.cnt[inc] += n
            tok = (inc, self.cnt[inc])
        self.q[eng].append((waits, fn, inc, n))
        return tok

    def dma(self, eng, out, in_, waits=(), sem="dS"):
        return self.emit(eng, lambda E, o=out, i=in_: E.dma_start(out=o, in_=i),
                         waits, sem, 16)

    def replay(self):
        with self.nc.Block() as block:
            for eng in self.ENGINES:
                queue = self.q[eng]
                sems = self.sem

                def body(E, queue=queue, sems=sems):
                    seen = {}
                    for waits, fn, inc, n in queue:
                        for s, v in waits:
                            if seen.get(s, 0) < v:
                                E.wait_ge(sems[s], v)
                                seen[s] = v
                        ins = fn(E)
                        if inc is not None:
                            ins.then_inc(sems[inc], n)

                getattr(block, eng)(body)


def _last(toks):
    best = {}
    for t in toks:
        if t is None:
            continue
        if isinstance(t, list):
            for tt in t:
                if tt is not None:
                    s, v = tt
                    if best.get(s, 0) < v:
                        best[s] = v
            continue
        s, v = t
        if best.get(s, 0) < v:
            best[s] = v
    return [(s, v) for s, v in best.items()]


def build_nc(debug=False):
    nc = bass.Bass("TRN2", target_bir_lowering=False)

    xfull = nc.dram_tensor("xfull", [S, D], F32, kind="ExternalInput").ap()
    xq = nc.dram_tensor("xq", [NTO * 128, D], F32, kind="ExternalInput").ap()
    wk = nc.dram_tensor("wk", [D, D], F32, kind="ExternalInput").ap()
    wvt = nc.dram_tensor("wvt", [D, D], F32, kind="ExternalInput").ap()
    wg = nc.dram_tensor("wg", [D, NEXP], F32, kind="ExternalInput").ap()
    bg = nc.dram_tensor("bg", [1, NEXP], F32, kind="ExternalInput").ap()
    wet = nc.dram_tensor("wet", [NEXP, D, D], F32, kind="ExternalInput").ap()
    bet = nc.dram_tensor("bet", [NEXP, D], F32, kind="ExternalInput").ap()
    masks = nc.dram_tensor("masks", [NTO, 128, MASKW], BF16, kind="ExternalInput").ap()
    ident = nc.dram_tensor("ident", [128, 128], F32, kind="ExternalInput").ap()
    out = nc.dram_tensor("out", [NTO * 128, D], F32, kind="ExternalOutput").ap()
    scratch_kind = "ExternalOutput" if debug else "Internal"
    xn_hbm = nc.dram_tensor("xn_hbm", [S, D], BF16, kind=scratch_kind).ap()
    g_hbm = nc.dram_tensor("g_hbm", [NTO * 128, D], BF16, kind=scratch_kind).ap()
    if debug:
        dbg_xqfm = nc.dram_tensor("dbg_xqfm", [128, NJ * NTO * 128], BF16,
                                  kind="ExternalOutput").ap()
        dbg_gatew = nc.dram_tensor("dbg_gatew", [128, NTO * NEXP], F32,
                                   kind="ExternalOutput").ap()
        dbg_qt = nc.dram_tensor("dbg_qt", [128, NTO * 1024], BF16,
                                kind="ExternalOutput").ap()
        dbg_z = nc.dram_tensor("dbg_z", [128, NTO * 1024], BF16,
                               kind="ExternalOutput").ap()
        dbg_st = nc.dram_tensor("dbg_st", [128, NTO * 1024], F32,
                                kind="ExternalOutput").ap()
        dbg_ast = nc.dram_tensor("dbg_ast", [128, NTO * 16], F32,
                                 kind="ExternalOutput").ap()

    with ExitStack() as ctx:
        E = ctx.enter_context
        P = Prog(nc, ctx)

        xn_fm = E(nc.sbuf_tensor("xn_fm", [128, NJ, S], BF16))
        xq_fm = E(nc.sbuf_tensor("xq_fm", [128, NJ, NTO * 128], BF16))
        wk_sb = E(nc.sbuf_tensor("wk_sb", [128, NJ, D], BF16))
        wvt_sb = E(nc.sbuf_tensor("wvt_sb", [128, NJ, D], BF16))
        idt = E(nc.sbuf_tensor("idt", [128, 128], BF16))
        wg_sb = E(nc.sbuf_tensor("wg_sb", [128, NJ, NEXP], BF16))
        bg_sb = E(nc.sbuf_tensor("bg_sb", [1, NEXP], BF16))
        be_sb = E(nc.sbuf_tensor("be_sb", [NEXP, D], BF16))
        ones1 = E(nc.sbuf_tensor("ones1", [1, 128], BF16))
        cst = E(nc.sbuf_tensor("cst", [128, 3], F32))
        stat_r = E(nc.sbuf_tensor("stat_r", [128, NTO], F32))
        stat_mr = E(nc.sbuf_tensor("stat_mr", [128, NTO], F32))
        mask_sb = E(nc.sbuf_tensor("mask_sb", [128, MASKW], BF16))
        lnstat_t = E(nc.sbuf_tensor("lnstat_t", [128, 16], F32))
        ast = E(nc.sbuf_tensor("ast", [128, 64], F32))

        ARENA_F32 = 19456
        arena = E(nc.sbuf_tensor("arena", [128, ARENA_F32], F32))
        a16 = arena.bitcast(BF16)

        def f32(off_kb, n):
            o = int(off_kb * KB)
            assert o % 4 == 0 and o + 4 * n <= ARENA_F32 * 4
            return arena[:, o // 4: o // 4 + n]

        def bf16(off_kb, n):
            o = int(off_kb * KB)
            assert o % 2 == 0 and o + 2 * n <= ARENA_F32 * 4
            return a16[:, o // 2: o // 2 + n]

        acc = f32(0, NTO * 512).rearrange("p (t n) -> p t n", t=NTO)
        xin = f32(32, 2 * D).rearrange("p (b n) -> p b n", b=2)
        xn_t = bf16(40, 4 * D).rearrange("p (b n) -> p b n", b=4)
        we_sb = bf16(48, 4 * 4 * 512).rearrange("p (b j n) -> p b j n", b=4, j=4)
        gate_w = f32(64, NTO * NEXP).rearrange("p (t e) -> p t e", t=NTO)
        gate_e = f32(64.5, NTO * NEXP).rearrange("p (t e) -> p t e", t=NTO)
        gmisc = f32(65, NTO * 16).rearrange("p (t c) -> p t c", t=NTO)
        wt_all = bf16(66, NTO * 128)
        g_stage = bf16(70, 2 * 512).rearrange("p (b n) -> p b n", b=2)
        lnstat = lnstat_t[:, :].rearrange("p (b c) -> p b c", b=2)
        scores = f32(0, SCW)
        p_sb = bf16(8, 4 * SCW).rearrange("p (b n) -> p b n", b=4)
        qbuf = bf16(24, D)
        qt_sb = bf16(26, NJ * 128)
        pt_sb = bf16(28, 4 * 256).rearrange("p (b n) -> p b n", b=4)
        z_acc = f32(30, D)
        z_sb = bf16(34, D)
        zt_sb = bf16(36, NJ * 128)
        kv_sb = bf16(38, KV_RING * D).rearrange("p (r n) -> p r n", r=KV_RING)
        xq_in = f32(54, 2 * D).rearrange("p (b n) -> p b n", b=2)
        g_in = bf16(62, 2 * D).rearrange("p (b n) -> p b n", b=2)
        out_sb = f32(66, D)
        xn_ep = f32(70, D)

        ps = [E(nc.psum_tensor(f"ps{i}", [128, 512], F32)) for i in range(8)]
        ps_ln = [ps[3].bitcast(BF16), ps[4].bitcast(BF16)]
        ps_qt = ps[2].bitcast(BF16)
        ps_wt = ps[2].bitcast(BF16)
        ps_pt = ps[7].bitcast(BF16)

        tok_id = P.dma("gpsimd", idt[:, :], ident[:, :], sem="dW")
        tok_wk = P.dma("gpsimd", wk_sb[:, :, :],
                       wk.rearrange("(c p) n -> p c n", p=128), sem="dW")
        tok_wvt = P.dma("gpsimd", wvt_sb[:, :, :],
                        wvt.rearrange("(c p) n -> p c n", p=128), sem="dW")
        tok_wg = P.dma("gpsimd", wg_sb[:, :, :],
                       wg.rearrange("(c p) e -> p c e", p=128), sem="dW")
        tok_bg = P.dma("gpsimd", bg_sb[:, :], bg[:, :], sem="dW")
        tok_be = P.dma("gpsimd", be_sb[:, :], bet[:, :], sem="dW")
        tok_ones = P.emit("gpsimd", lambda E: E.memset(ones1[:, :], 1.0), (), "PO")
        P.emit("gpsimd", lambda E: E.memset(cst[:, 0:1], 1.0 / D))
        P.emit("gpsimd", lambda E: E.memset(cst[:, 1:2], EPS))
        tok_cst = P.emit("gpsimd", lambda E: E.memset(cst[:, 2:3], -0.5), (), "PO")

        xin_free = [[], []]
        xnt_free = [[], [], [], []]
        lnstat_free = [None, None]
        psln_free = [None, None]
        ln_fm_tok = []
        xqfm_tok = [None] * NTO
        stat_tok = [None] * NTO
        gtile_ctr = [0]

        def ln_src(gi):
            if gi >= 32:
                return xq[(gi - 32) * 128:(gi - 31) * 128, :]
            return xfull[gi * 128:(gi + 1) * 128, :]

        sum_toks = {}

        def ln_load_sums(gi):
            par = gi % 2
            par4 = gi % 4
            d_tok = P.dma("sync", xin[:, par, :],
                          ln_src(gi), waits=_last(xin_free[par]), sem="dS")
            sum_toks[gi] = P.emit("vector", lambda E, a=lnstat[:, par, :],
                                  b=xin[:, par, :]:
                                  E.reduce_sum(a[:, 0:1], b, axis=AX, negate=True),
                                  waits=_last([d_tok, lnstat_free[par]]), inc="VE")

        def ln_stats_norm(gi):
            par = gi % 2
            par4 = gi % 4
            st = lnstat[:, par, :]
            xt = xin[:, par, :]
            t_nm = P.emit("scalar", lambda E, a=st:
                          E.mul(a[:, 1:2], a[:, 0:1], 1.0 / D),
                          waits=[sum_toks[gi]], inc="AC")
            t_ssq = P.emit("scalar", lambda E, a=st, b=xt, o=xn_t[:, par4, :]:
                           E.activation(out=o, in_=b, func=AF.Square,
                                        bias=a[:, 1:2], accum_out=a[:, 2:3]),
                           waits=_last([t_nm] + xnt_free[par4]), inc="AC")
            t_std = P.emit("scalar", lambda E, a=st:
                           E.activation(out=a[:, 3:4], in_=a[:, 2:3], func=AF.Sqrt,
                                        scale=cst[:, 0:1], bias=cst[:, 1:2]),
                           waits=[t_ssq, tok_cst], inc="AC")
            t_rstd = P.emit("vector", lambda E, a=st:
                            E.reciprocal(a[:, 4:5], a[:, 3:4]),
                            waits=[t_std], inc="VE")
            t_nmr = P.emit("vector", lambda E, a=st:
                           E.tensor_tensor(a[:, 5:6], a[:, 1:2], a[:, 4:5],
                                           ALU.mult),
                           waits=[t_rstd], inc="VE")
            t_xn = P.emit("scalar", lambda E, a=st, b=xt, o=xn_t[:, par4, :]:
                          E.activation(out=o, in_=b, func=AF.Identity,
                                       scale=a[:, 4:5], bias=a[:, 5:6]),
                          waits=_last([t_nmr, t_ssq]), inc="AC")
            lnstat_free[par] = t_xn
            if gi >= 32:
                sid = gi - 32
                t_sr = P.emit("vector", lambda E, a=st, sid=sid:
                              E.tensor_copy(stat_r[:, sid:sid + 1], a[:, 4:5]),
                              waits=[t_rstd], inc="VE")
                t_smr = P.emit("vector", lambda E, a=st, sid=sid:
                               E.tensor_tensor(stat_mr[:, sid:sid + 1],
                                               a[:, 1:2], a[:, 4:5], ALU.mult),
                               waits=[t_sr], inc="VE")
                stat_tok[sid] = t_smr
            xin_free[par] = [t_ssq, t_xn]
            return t_xn, t_ssq

        def ln_transpose_store(gi, t_xn, t_ssq):
            par4 = gi % 4
            own = gi >= 32
            bpar = gtile_ctr[0] % 2
            tr_tok = None
            for j in range(NJ):
                tr_tok = P.emit("tensor",
                                lambda E, j=j, b=bpar, src=xn_t[:, par4, :]:
                                E.transpose(ps_ln[b][:, j * 128:(j + 1) * 128],
                                            src[:, j * 128:(j + 1) * 128],
                                            idt[:, :]),
                                waits=_last([t_xn, tok_id, psln_free[bpar]]),
                                inc="PE" if j == NJ - 1 else None)
            if own:
                sid = gi - 32
                dest = xq_fm[:, :, sid * 128:(sid + 1) * 128]
            else:
                dest = xn_fm[:, :, gi * 128:(gi + 1) * 128]
            cp_eng = "vector" if (gtile_ctr[0] % 2 == 0) else "scalar"

            def _cp(E, d=dest, b=bpar):
                src = ps_ln[b][:, :].rearrange("p (j t) -> p j t", j=NJ)
                return (E.tensor_copy(d, src) if hasattr(E, "tensor_copy")
                        else E.copy(d, src))
            t_cp = P.emit(cp_eng, _cp, waits=[tr_tok],
                          inc="VE" if cp_eng == "vector" else "AC")
            psln_free[bpar] = t_cp
            if own:
                xqfm_tok[gi - 32] = t_cp
            else:
                ln_fm_tok.append(t_cp)
            gtile_ctr[0] += 1
            if not own:
                s_tok = P.dma("scalar",
                              xn_hbm[gi * 128:(gi + 1) * 128, :],
                              xn_t[:, par4, :], waits=[t_xn], sem="dQ")
                xnt_free[par4] = [s_tok, tr_tok]
            else:
                xnt_free[par4] = [tr_tok]

        pairs = [(h, e) for h in range(2) for e in range(NEXP)]
        tok_we = [None] * 4
        we_free = [None] * 4

        def emit_we_pair(i):
            h, e = pairs[i]
            for jh in range(2):
                buf = (2 * i + jh) % 4
                src = wet[e, jh * 512:(jh + 1) * 512,
                          h * 512:(h + 1) * 512].rearrange(
                    "(c p) n -> p c n", p=128)
                tok_we[buf] = P.dma("gpsimd", we_sb[:, buf, :, :], src,
                                    waits=_last([we_free[buf]]), sem="dW")

        emit_we_pair(0)
        emit_we_pair(1)

        ln_load_sums(32)
        for gi in range(32, 48):
            if gi + 1 < 48:
                ln_load_sums(gi + 1)
            t_xn, t_ssq = ln_stats_norm(gi)
            ln_transpose_store(gi, t_xn, t_ssq)

        gl = ps[2][:, 0:NTO * NEXP].rearrange("p (t e) -> p t e", t=NTO)
        mm_tok = [None] * NTO
        for t in range(NTO):
            for j in range(NJ):
                P.emit("tensor", lambda E, j=j, t=t:
                       E.matmul(gl[:, t, :],
                                xq_fm[:, j, t * 128:(t + 1) * 128],
                                wg_sb[:, j, :], start=(j == 0), stop=False),
                       waits=_last([xqfm_tok[t], tok_wg, tok_bg, tok_ones, tok_id]))
            mm_tok[t] = P.emit("tensor", lambda E, t=t:
                               E.matmul(gl[:, t, :], ones1[0:1, :], bg_sb[0:1, :],
                                        start=False, stop=True), inc="PE")
        tg = [None] * NTO
        for t in range(NTO):
            tg[t] = P.emit("vector", lambda E, t=t:
                           E.reduce_max(gmisc[:, t, 0:1], gl[:, t, :], axis=AX,
                                        negate=True),
                           waits=[mm_tok[NTO - 1]], inc="VE")
        te = [None] * NTO
        for t in range(NTO):
            te[t] = P.emit("scalar", lambda E, t=t:
                           E.activation(out=gate_e[:, t, :], in_=gl[:, t, :],
                                        func=AF.Exp, bias=gmisc[:, t, 0:1]),
                           waits=[tg[t]], inc="AC")
        t1 = [P.emit("vector", lambda E, t=t:
                     E.reduce_max(gmisc[:, t, 1:2], gate_e[:, t, :], axis=AX),
                     waits=[te[t]], inc="VE") for t in range(NTO)]
        t2 = [P.emit("vector", lambda E, t=t:
                     E.tensor_scalar(gmisc[:, t, 8:16], gate_e[:, t, :],
                                     gmisc[:, t, 1:2], 0.0, ALU.is_ge, ALU.add),
                     waits=[t1[t]], inc="VE") for t in range(NTO)]
        t3 = [P.emit("vector", lambda E, t=t:
                     E.scalar_tensor_tensor(gmisc[:, t, 8:16], gmisc[:, t, 8:16],
                                            -1e4, gate_e[:, t, :],
                                            ALU.mult, ALU.add),
                     waits=[t2[t]], inc="VE") for t in range(NTO)]
        t4 = [P.emit("vector", lambda E, t=t:
                     E.reduce_max(gmisc[:, t, 2:3], gmisc[:, t, 8:16], axis=AX),
                     waits=[t3[t]], inc="VE") for t in range(NTO)]
        t5 = [P.emit("vector", lambda E, t=t:
                     E.tensor_tensor(gmisc[:, t, 3:4], gmisc[:, t, 1:2],
                                     gmisc[:, t, 2:3], ALU.add),
                     waits=[t4[t]], inc="VE") for t in range(NTO)]
        t6 = [P.emit("vector", lambda E, t=t:
                     E.reciprocal(gmisc[:, t, 4:5], gmisc[:, t, 3:4]),
                     waits=[t5[t]], inc="VE") for t in range(NTO)]
        t7 = [P.emit("vector", lambda E, t=t:
                     E.tensor_scalar(gmisc[:, t, 8:16], gate_e[:, t, :],
                                     gmisc[:, t, 2:3], 0.0, ALU.is_ge, ALU.add),
                     waits=[t4[t]], inc="VE") for t in range(NTO)]
        t8 = [P.emit("vector", lambda E, t=t:
                     E.tensor_tensor(gmisc[:, t, 8:16], gmisc[:, t, 8:16],
                                     gate_e[:, t, :], ALU.mult),
                     waits=[t7[t]], inc="VE") for t in range(NTO)]
        gatew_tok = [None] * NTO
        for t in range(NTO):
            gatew_tok[t] = P.emit("vector", lambda E, t=t:
                                  E.tensor_scalar(gate_w[:, t, :],
                                                  gmisc[:, t, 8:16],
                                                  gmisc[:, t, 4:5],
                                                  0.0, ALU.mult, ALU.add),
                                  waits=[t8[t], t6[t]], inc="VE")
        wtall_tok = [None] * NTO
        p2_free = te[NTO - 1]

        def emit_wt_all():
            nonlocal p2_free
            tb16 = [P.emit("vector", lambda E, t=t:
                           E.tensor_copy(g_stage[:, 0, t * 8:(t + 1) * 8],
                                         gate_w[:, t, :]),
                           waits=[gatew_tok[t]], inc="VE") for t in range(NTO)]
            for t in range(NTO):
                t11 = P.emit("tensor", lambda E, t=t:
                             E.transpose(ps_wt[0:8, 256:384],
                                         g_stage[:, 0, t * 8:(t + 1) * 8],
                                         idt[:, :]),
                             waits=_last([tb16[t], p2_free] + te), inc="PE")
                t12 = P.emit("scalar", lambda E, t=t:
                             E.copy(wt_all[0:8, t * 128:(t + 1) * 128],
                                    ps_wt[0:8, 256:384]),
                             waits=[t11], inc="AC")
                wtall_tok[t] = t12
                p2_free = t12

        EXP_BANKS = [0, 1, 5, 6]
        psy_free = [None, None, None, None]
        acc_tok = [None] * NTO
        acc_free = [None] * NTO
        g_stage_free = [None, None]
        g_store_tok = []
        last_bias_mm = None

        ln_pending = []
        ln_load_sums(0)
        ln_load_sums(1)

        def ln_full_step(k):
            if k < 32:
                ln_pending.append((k, ln_stats_norm(k)))

        def ln_full_flush(upto):
            while ln_pending and ln_pending[0][0] <= upto:
                k, (t_xn, t_ssq) = ln_pending.pop(0)
                ln_transpose_store(k, t_xn, t_ssq)

        for i, (h, e) in enumerate(pairs):
            if i + 1 < len(pairs) and i >= 1:
                emit_we_pair(i + 1)
            if i == 1:
                emit_wt_all()
            ln_full_flush(2 * i - 1)
            ln_full_step(2 * i)
            ln_full_step(2 * i + 1)
            last_mm = None
            for t in range(NTO):
                if t == 8:
                    for k in (2 * i + 2, 2 * i + 3):
                        if k < 32:
                            ln_load_sums(k)
                bpar = t % 4
                bank = EXP_BANKS[bpar]
                mm = None
                for j in range(NJ):
                    jh, jr = (0, j) if j < 4 else (1, j - 4)
                    buf = (2 * i + jh) % 4
                    mm = P.emit("tensor", lambda E, j=j, t=t, b=bank, bu=buf, jr=jr:
                                E.matmul(ps[b][:, :],
                                         xq_fm[:, j, t * 128:(t + 1) * 128],
                                         we_sb[:, bu, jr, :],
                                         start=(j == 0), stop=(j == NJ - 1)),
                                waits=_last([tok_we[buf], psy_free[bpar],
                                             xqfm_tok[t]]),
                                inc="PE" if j == NJ - 1 else None)
                last_mm = mm
                if e == 0:
                    tv = P.emit("vector", lambda E, t=t, b=bank:
                                E.tensor_scalar(acc[:, t, :], ps[b][:, :],
                                                gate_w[:, t, 0:1], 0.0,
                                                ALU.mult, ALU.add),
                                waits=_last([mm, gatew_tok[t], acc_free[t]]),
                                inc="VE")
                else:
                    tv = P.emit("vector", lambda E, t=t, b=bank, e=e:
                                E.scalar_tensor_tensor(acc[:, t, :], ps[b][:, :],
                                                       gate_w[:, t, e:e + 1],
                                                       acc[:, t, :],
                                                       ALU.mult, ALU.add),
                                waits=_last([mm, acc_tok[t]]), inc="VE")
                acc_tok[t] = tv
                psy_free[bpar] = tv
            we_free[(2 * i) % 4] = last_mm
            we_free[(2 * i + 1) % 4] = last_mm
            if e == NEXP - 1:
                for t in range(NTO):
                    mmb = P.emit("tensor", lambda E, t=t, h=h:
                                 E.matmul(ps[2][:, :],
                                          wt_all[0:8, t * 128:(t + 1) * 128],
                                          be_sb[0:8, h * 512:(h + 1) * 512],
                                          start=True, stop=True),
                                 waits=_last([wtall_tok[t], tok_be, p2_free]),
                                 inc="PE")
                    last_bias_mm = mmb
                    tb = P.emit("vector", lambda E, t=t:
                                E.tensor_tensor(acc[:, t, :], acc[:, t, :],
                                                ps[2][:, :], ALU.add),
                                waits=_last([mmb, acc_tok[t]]), inc="VE")
                    p2_free = tb
                    acc_tok[t] = tb
                    gpar = t % 2
                    tsil = P.emit("scalar", lambda E, t=t, b=gpar:
                                  E.activation(out=g_stage[:, b, :],
                                               in_=acc[:, t, :], func=AF.Silu),
                                  waits=_last([tb, g_stage_free[gpar]]), inc="AC")
                    gd = P.dma("gpsimd",
                               g_hbm[t * 128:(t + 1) * 128, h * 512:(h + 1) * 512],
                               g_stage[:, gpar, :], waits=[tsil], sem="dW")
                    g_stage_free[gpar] = gd
                    g_store_tok.append(gd)
                    acc_free[t] = tsil
        ln_full_flush(31)

        xn_stores_done = ("dQ", P.cnt["dQ"])
        ln_done = _last(ln_fm_tok + xqfm_tok + [psln_free[0], psln_free[1]])
        ln_xnt_done = _last(xnt_free[0] + xnt_free[1])

        if debug:
            P.dma("sync", dbg_xqfm[:, :].rearrange("p (j n) -> p j n", j=NJ),
                  xq_fm[:, :, :], waits=ln_done, sem="dS")

        moe_done = _last(acc_tok + g_store_tok + [p2_free, psy_free[0], psy_free[1],
                                                  last_bias_mm] + acc_free +
                         ln_done + ln_xnt_done + [xn_stores_done])

        if debug:
            P.dma("sync", dbg_gatew[:, :].rearrange("p (t e) -> p t e", t=NTO),
                  gate_w[:, :, :], waits=moe_done, sem="dS")

        def nchunks(s):
            return (256 * (s + 1) + SCW - 1) // SCW

        mask_free = [None]
        mask_tok = [None] * NTO
        xqin_free = [None, None]
        xqin_tok = [None] * NTO
        gin_free = [None, None]
        gin_tok = [None] * NTO
        out_free = [None]
        out_dma_toks = []

        kv_ring_free = [None] * KV_RING
        kv_cnt = [0]

        p01_free = [psy_free[0], psy_free[1]]
        p34_free = [psln_free[0], psln_free[1]]
        p56_free = [_last([psy_free[2], psy_free[3]])[0]]
        p2qt_free = [p2_free]
        p7_free = [None]
        ppt_free = [None, None]
        ptsb_free = [None] * 4
        qt_free = [None]
        qbuf_free = [None]

        sc_free = [None]
        p_free = [None] * 4
        pexp_tok = {}
        gsc = [0]
        slot_state = {}

        def prefetch_mask(s):
            barrier = list(moe_done) if s < 2 else []
            mask_tok[s] = P.dma("scalar", mask_sb[:, :], masks[s, :, :],
                                waits=_last([mask_free[0]] + barrier), sem="dQ")

        def prefetch_slot(s):
            par = s % 2
            barrier = list(moe_done) if s < 2 else []
            xqin_tok[s] = P.dma("gpsimd", xq_in[:, par, :],
                                xq[s * 128:(s + 1) * 128, :],
                                waits=_last([xqin_free[par]] + barrier), sem="dW")
            gin_tok[s] = P.dma("gpsimd", g_in[:, par, :],
                               g_hbm[s * 128:(s + 1) * 128, :],
                               waits=_last([gin_free[par], g_store_tok[s],
                                            g_store_tok[NTO + s]] + barrier),
                               sem="dW")

        def emit_A1(s):
            toks = []
            for half in range(2):
                mm = None
                for j in range(NJ):
                    mm = P.emit("tensor", lambda E, j=j, h=half, s=s:
                                E.matmul(ps[h][:, :],
                                         xq_fm[:, j, s * 128:(s + 1) * 128],
                                         wk_sb[:, j, h * 512:(h + 1) * 512],
                                         start=(j == 0), stop=(j == NJ - 1)),
                                waits=_last([tok_wk, xqfm_tok[s], p01_free[half]]),
                                inc="PE" if j == NJ - 1 else None)
                toks.append(mm)
            ev = []
            for half in range(2):
                e = P.emit("scalar", lambda E, h=half:
                           E.copy(qbuf[:, h * 512:(h + 1) * 512], ps[h][:, :]),
                           waits=_last([toks[half], qbuf_free[0]]), inc="AC")
                ev.append(e)
            p01_free[0] = ev[0]
            p01_free[1] = ev[1]
            return ev

        def emit_A2(s, ev):
            trt = None
            for j in range(NJ):
                trt = P.emit("tensor", lambda E, j=j:
                             E.transpose(ps_qt[:, j * 128:(j + 1) * 128],
                                         qbuf[:, j * 128:(j + 1) * 128], idt[:, :]),
                             waits=_last(ev + [p2qt_free[0]]),
                             inc="PE" if j == NJ - 1 else None)
            qbuf_free[0] = trt
            cq = P.emit("vector", lambda E:
                        E.tensor_copy(qt_sb[:, :], ps_qt[:, :]),
                        waits=_last([trt, qt_free[0]]), inc="VE")
            p2qt_free[0] = cq
            if debug:
                P.dma("sync", dbg_qt[:, s * 1024:(s + 1) * 1024], qt_sb[:, :],
                      waits=[cq], sem="dS")
            return cq

        def emit_S(s, c, cq_tok):
            W = min(SCW, 256 * (s + 1) - c * SCW)
            nsub = (W + 511) // 512
            ab = (s % 4) * 16
            last_chunk = (c == nchunks(s) - 1)
            evs = []
            submax = []
            last_mm = None
            for w in range(nsub):
                n = min(512, W - w * 512)
                bank = 3 + (w % 2)
                mm = None
                for j in range(NJ):
                    mm = P.emit("tensor",
                                lambda E, j=j, w=w, n=n, bank=bank, s=s, c=c:
                                E.matmul(ps[bank][:, 0:n],
                                         qt_sb[:, j * 128:(j + 1) * 128],
                                         xn_fm[:, j, c * SCW + w * 512:
                                               c * SCW + w * 512 + n],
                                         start=(j == 0), stop=(j == NJ - 1)),
                                waits=_last([cq_tok, p34_free[bank - 3]] + ln_done),
                                inc="PE" if j == NJ - 1 else None)
                last_mm = mm
                evac = P.emit("scalar", lambda E, w=w, n=n, bank=bank:
                              E.copy(scores[:, w * 512:w * 512 + n],
                                     ps[bank][:, 0:n]),
                              waits=_last([mm, sc_free[0]]), inc="AC")
                p34_free[bank - 3] = evac
                evs.append(evac)
                mwait = [evac]
                if last_chunk and w == nsub - 1:
                    mtok = P.emit("vector", lambda E, W=W, s=s:
                                  E.tensor_tensor(scores[:, W - MASKW:W],
                                                  scores[:, W - MASKW:W],
                                                  mask_sb[:, :], ALU.add),
                                  waits=_last([evac, mask_tok[s], mask_free[0]]),
                                  inc="VE")
                    mask_free[0] = mtok
                    mwait = [mtok]
                sm_t = P.emit("vector", lambda E, w=w, n=n, ab=ab:
                              E.reduce_max(ast[:, ab + 9 + w:ab + 10 + w],
                                           scores[:, w * 512:w * 512 + n],
                                           axis=AX, negate=True),
                              waits=_last(mwait), inc="VE")
                submax.append(sm_t)
            return evs, submax, last_mm

        def emit_softmax(s, c, evs, submax):
            W = min(SCW, 256 * (s + 1) - c * SCW)
            nsub = (W + 511) // 512
            par4 = gsc[0] % 4
            ab = (s % 4) * 16
            cm = submax[0]
            if nsub == 1:
                cm = P.emit("vector", lambda E, ab=ab, c=c:
                            E.tensor_copy(ast[:, ab + c:ab + c + 1],
                                          ast[:, ab + 9:ab + 10]),
                            waits=[submax[0]], inc="VE")
            else:
                for w in range(1, nsub):
                    src0 = (9 if w == 1 else c)
                    cm = P.emit("vector", lambda E, ab=ab, c=c, w=w, src0=src0:
                                E.tensor_tensor(ast[:, ab + c:ab + c + 1],
                                                ast[:, ab + src0:ab + src0 + 1],
                                                ast[:, ab + 9 + w:ab + 10 + w],
                                                ALU.min),
                                waits=_last([submax[w], cm]), inc="VE")
            if c == 0:
                bias_tok, bias_col = cm, ast[:, ab + 0:ab + 1]
            else:
                nm_t = P.emit("vector", lambda E, ab=ab:
                              E.tensor_tensor(ast[:, ab + 4:ab + 5],
                                              ast[:, ab + 0:ab + 1],
                                              ast[:, ab + 1:ab + 2], ALU.min),
                              waits=[cm], inc="VE")
                slot_state[s]["nm_tok"] = nm_t
                d0 = P.emit("vector", lambda E, ab=ab:
                            E.tensor_tensor(ast[:, ab + 5:ab + 6],
                                            ast[:, ab + 4:ab + 5],
                                            ast[:, ab + 0:ab + 1], ALU.subtract),
                            waits=[nm_t], inc="VE")
                a0 = P.emit("scalar", lambda E, ab=ab:
                            E.activation(out=ast[:, ab + 6:ab + 7],
                                         in_=ast[:, ab + 5:ab + 6], func=AF.Exp),
                            waits=[d0], inc="AC")
                slot_state[s]["a0_tok"] = a0
                bias_tok, bias_col = nm_t, ast[:, ab + 4:ab + 5]
            ex = P.emit("scalar", lambda E, W=W, b=bias_col, c=c, ab=ab, par4=par4:
                        E.activation(out=p_sb[:, par4, 0:W], in_=scores[:, 0:W],
                                     func=AF.Exp, bias=b,
                                     accum_out=ast[:, ab + 2 + c:ab + 3 + c]),
                        waits=_last([bias_tok, p_free[par4]]), inc="AC")
            pexp_tok[(s, c)] = ex
            sc_free[0] = ex
            return ex

        kv_map = {}
        kv_primed = {}

        def kv_dma(s, c, kt0, n):
            pos = kv_cnt[0] % KV_RING
            n = min(n, KV_RING - pos)
            gkt0 = c * (SCW // 128) + kt0
            rows = xn_hbm[gkt0 * 128:(gkt0 + n) * 128, :]
            queue = "sync" if (kv_cnt[0] // KV_CHUNK) % 2 == 0 else "scalar"
            waits = _last([kv_ring_free[pos + i] for i in range(n)] +
                          [xn_stores_done] +
                          (list(moe_done) if kv_cnt[0] < KV_RING else []))
            tok = P.dma(queue, kv_sb[:, pos:pos + n, :],
                        rows.rearrange("(c p) n -> p c n", p=128),
                        waits=waits, sem="dS" if queue == "sync" else "dQ")
            for i in range(n):
                kv_map[(s, c, kt0 + i)] = (tok, pos + i)
            kv_cnt[0] += n
            return n

        def kv_prime(s):
            nkt = min(2 * (s + 1), SCW // 128)
            loaded = 0
            while loaded < min(nkt, KV_RING):
                loaded += kv_dma(s, 0, loaded,
                                 min(KV_CHUNK, nkt - loaded, KV_RING - loaded))
            kv_primed[s] = loaded

        def emit_Z(s, c):
            W = min(SCW, 256 * (s + 1) - c * SCW)
            nkt = W // 128
            par4 = slot_state[s]["sc_par"][c]
            ex = pexp_tok[(s, c)]
            ngrp = (nkt + 1) // 2
            cp_tok = [None] * ngrp
            pt_banks = [ps_pt, ps_qt]

            def z_group(g):
                kt0 = g * 2
                n = min(2, nkt - kt0)
                last = None
                for i in range(n):
                    kt = kt0 + i
                    dtok, ring = kv_map[(s, c, kt)]
                    for half in range(2):
                        last = P.emit(
                            "tensor",
                            lambda E, i=i, h=half, r=ring, g=g, kt=kt, nkt=nkt:
                            E.matmul(ps[5 + h][:, :],
                                     pt_sb[:, g % 4, i * 128:(i + 1) * 128],
                                     kv_sb[:, r, h * 512:(h + 1) * 512],
                                     start=(kt == 0), stop=(kt == nkt - 1)),
                            waits=_last([cp_tok[g], dtok] +
                                        ([p56_free[0]] if kt == 0 else [])),
                            inc="PE" if half == 1 else None)
                    kv_ring_free[ring] = last
                return last

            if c == 0:
                kv_loaded = kv_primed.get(s, 0)
            else:
                kv_loaded = 0
            while kv_loaded < min(nkt, KV_RING):
                kv_loaded += kv_dma(s, c, kv_loaded,
                                    min(KV_CHUNK, nkt - kv_loaded,
                                        KV_RING - kv_loaded))
            last_z = None

            def tr_group(g):
                kt0 = g * 2
                n = min(2, nkt - kt0)
                bankt = pt_banks[g % 2]
                tr = None
                for i in range(n):
                    kt = kt0 + i
                    tr = P.emit("tensor", lambda E, i=i, kt=kt, par4=par4, bk=bankt:
                                E.transpose(bk[:, i * 128:(i + 1) * 128],
                                            p_sb[:, par4, kt * 128:(kt + 1) * 128],
                                            idt[:, :]),
                                waits=_last([ex, ppt_free[g % 2]]),
                                inc="PE" if i == n - 1 else None)
                cp = P.emit("vector", lambda E, g=g, n=n, bk=bankt:
                            E.tensor_copy(pt_sb[:, g % 4, 0:n * 128],
                                          bk[:, 0:n * 128]),
                            waits=_last([tr, ptsb_free[g % 4]]), inc="VE")
                cp_tok[g] = cp
                ppt_free[g % 2] = cp
                if g % 2 == 0:
                    p7_free[0] = cp

            for g in range(min(2, ngrp)):
                tr_group(g)
            for g in range(ngrp):
                if g + 2 < ngrp:
                    tr_group(g + 2)
                z = z_group(g)
                ptsb_free[g % 4] = z
                last_z = z
                if g >= 1 and kv_loaded < nkt:
                    kv_loaded += kv_dma(s, c, kv_loaded, min(2, nkt - kv_loaded))
            p_free[par4] = last_z
            return last_z

        def emit_zfin(s):
            st = slot_state[s]
            ab = (s % 4) * 16
            nch = nchunks(s)
            if nch == 1:
                rs = P.emit("vector", lambda E, ab=ab:
                            E.reciprocal(ast[:, ab + 7:ab + 8],
                                         ast[:, ab + 2:ab + 3]),
                            waits=[pexp_tok[(s, 0)]], inc="VE")
                zs = []
                for half in range(2):
                    zz = P.emit("vector", lambda E, h=half, ab=ab:
                                E.tensor_scalar(z_sb[:, h * 512:(h + 1) * 512],
                                                ps[5 + h][:, :],
                                                ast[:, ab + 7:ab + 8], 0.0,
                                                ALU.mult, ALU.add),
                                waits=_last([st["z_toks"][-1], rs,
                                             st["zsb_free"]]),
                                inc="VE")
                    zs.append(zz)
                z_ready = zs
            else:
                a0 = st["a0_tok"]
                zts = []
                for half in range(2):
                    zt = P.emit("vector", lambda E, h=half, ab=ab:
                                E.scalar_tensor_tensor(
                                    z_acc[:, h * 512:(h + 1) * 512],
                                    z_acc[:, h * 512:(h + 1) * 512],
                                    ast[:, ab + 6:ab + 7], ps[5 + h][:, :],
                                    ALU.mult, ALU.add),
                                waits=_last([a0, st["z_toks"][-1],
                                             st["zacc_tok"]]),
                                inc="VE")
                    zts.append(zt)
                sm = P.emit("vector", lambda E, ab=ab:
                            E.scalar_tensor_tensor(ast[:, ab + 8:ab + 9],
                                                   ast[:, ab + 2:ab + 3],
                                                   ast[:, ab + 6:ab + 7],
                                                   ast[:, ab + 3:ab + 4],
                                                   ALU.mult, ALU.add),
                            waits=_last([a0, pexp_tok[(s, 1)]]), inc="VE")
                rs = P.emit("vector", lambda E, ab=ab:
                            E.reciprocal(ast[:, ab + 7:ab + 8],
                                         ast[:, ab + 8:ab + 9]),
                            waits=[sm], inc="VE")
                zz = P.emit("vector", lambda E, ab=ab:
                            E.tensor_scalar(z_sb[:, :], z_acc[:, :],
                                            ast[:, ab + 7:ab + 8], 0.0,
                                            ALU.mult, ALU.add),
                            waits=_last(zts + [rs, st["zsb_free"]]), inc="VE")
                z_ready = [zz]
            st["z_ready"] = z_ready

        def emit_FIN_ZTAV(s):
            st = slot_state[s]
            ab = (s % 4) * 16
            z_ready = st["z_ready"]
            trt = None
            for j in range(NJ):
                trt = P.emit("tensor", lambda E, j=j:
                             E.transpose(ps_pt[:, j * 128:(j + 1) * 128],
                                         z_sb[:, j * 128:(j + 1) * 128], idt[:, :]),
                             waits=_last(z_ready + [p7_free[0]]),
                             inc="PE" if j == NJ - 1 else None)
            czt = P.emit("vector", lambda E:
                         E.tensor_copy(zt_sb[:, :], ps_pt[:, :]),
                         waits=_last([trt, st["zt_free"]]), inc="VE")
            p7_free[0] = czt
            av = []
            for half in range(2):
                mm = None
                for j in range(NJ):
                    mm = P.emit("tensor", lambda E, j=j, h=half:
                                E.matmul(ps[5 + h][:, :],
                                         zt_sb[:, j * 128:(j + 1) * 128],
                                         wvt_sb[:, j, h * 512:(h + 1) * 512],
                                         start=(j == 0), stop=(j == NJ - 1)),
                                waits=_last([czt, tok_wvt] + z_ready),
                                inc="PE" if j == NJ - 1 else None)
                av.append(mm)
            if debug:
                P.dma("sync", dbg_z[:, s * 1024:(s + 1) * 1024], z_sb[:, :],
                      waits=_last(z_ready), sem="dS")
                P.dma("sync", dbg_ast[:, s * 16:(s + 1) * 16],
                      ast[:, ab:ab + 16], waits=_last(z_ready), sem="dS")
            st["av_toks"] = av

        def emit_EP(s):
            st = slot_state[s]
            av = st["av_toks"]
            xe = P.emit("scalar", lambda E, s=s:
                        E.activation(out=xn_ep[:, :], in_=xq_in[:, s % 2, :],
                                     func=AF.Identity, scale=stat_r[:, s:s + 1],
                                     bias=stat_mr[:, s:s + 1]),
                        waits=_last([xqin_tok[s], stat_tok[s], st["xnep_free"]]),
                        inc="AC")
            sts = []
            for half in range(2):
                t = P.emit("vector", lambda E, h=half:
                           E.tensor_tensor(xn_ep[:, h * 512:(h + 1) * 512],
                                           ps[5 + h][:, :],
                                           xn_ep[:, h * 512:(h + 1) * 512], ALU.add),
                           waits=_last([av[h], xe]), inc="VE")
                sts.append(t)
            dbg_tok = None
            if debug:
                dbg_tok = P.dma("sync", dbg_st[:, s * 1024:(s + 1) * 1024],
                                xn_ep[:, :], waits=_last(sts), sem="dS")
            sg = P.emit("vector", lambda E, s=s:
                        E.tensor_tensor(xn_ep[:, :], xn_ep[:, :],
                                        g_in[:, s % 2, :], ALU.mult),
                        waits=_last(sts + [gin_tok[s], dbg_tok]), inc="VE")
            ob = P.emit("vector", lambda E, s=s:
                        E.tensor_tensor(out_sb[:, :], xn_ep[:, :],
                                        xq_in[:, s % 2, :], ALU.add),
                        waits=_last([sg, out_free[0]]), inc="VE")
            od = P.dma("gpsimd", out[s * 128:(s + 1) * 128, :], out_sb[:, :],
                       waits=[ob], sem="dW")
            out_free[0] = od
            xqin_free[s % 2] = ob
            gin_free[s % 2] = ob
            return sts, sg, ob, od

        fin_prev = {"zsb": None, "zt": None, "xnep": None}

        def emit_AS(s):
            prefetch_mask(s)
            prefetch_slot(s)
            slot_state[s] = {"sc_par": {}, "z_toks": [], "S_last_mm": None}
            ev = emit_A1(s)
            cq = emit_A2(s, ev)
            slot_state[s]["cq"] = cq
            emit_Ssoft(s)

        def emit_Ssoft(s):
            cq = slot_state[s]["cq"]
            for c in range(nchunks(s)):
                slot_state[s]["sc_par"][c] = gsc[0] % 4
                evs, submax, last_mm = emit_S(s, c, cq)
                slot_state[s]["S_last_mm"] = last_mm
                emit_softmax(s, c, evs, submax)
                gsc[0] += 1
            qt_free[0] = slot_state[s]["S_last_mm"]

        def emit_UB(s):
            st = slot_state[s]
            st["zsb_free"] = fin_prev["zsb"]
            st["zt_free"] = fin_prev["zt"]
            st["xnep_free"] = fin_prev["xnep"]
            ppt_free[0] = p7_free[0]
            ppt_free[1] = p2qt_free[0]
            nch = nchunks(s)
            for c in range(nch):
                z = emit_Z(s, c)
                st["z_toks"].append(z)
                if nch > 1 and c == 0:
                    zes = []
                    for half in range(2):
                        ze = P.emit("vector", lambda E, h=half:
                                    E.tensor_copy(z_acc[:, h * 512:(h + 1) * 512],
                                                  ps[5 + h][:, :]),
                                    waits=[z], inc="VE")
                        zes.append(ze)
                    st["zacc_tok"] = zes[-1]
                    p56_free[0] = zes[-1]
            emit_zfin(s)
            p2qt_free[0] = ppt_free[1]
            if s + 2 < NTO:
                slot_state[s + 2] = {"sc_par": {}, "z_toks": [],
                                     "S_last_mm": None}
                ev2 = emit_A1(s + 2)
            emit_FIN_ZTAV(s)
            if s + 2 < NTO:
                cq = emit_A2(s + 2, ev2)
                slot_state[s + 2]["cq"] = cq
                prefetch_mask(s + 2)
            sts, sg, ob, od = emit_EP(s)
            p56_free[0] = _last(sts)[0]
            if s + 2 < NTO:
                emit_Ssoft(s + 2)
                prefetch_slot(s + 2)
            if s + 1 < NTO:
                kv_prime(s + 1)
            out_dma_toks.append(od)
            fin_prev["zsb"] = [("PE", P.cnt["PE"])] + \
                ([("dS", P.cnt["dS"])] if debug else [])
            fin_prev["zt"] = ("PE", P.cnt["PE"])
            fin_prev["xnep"] = ob

        emit_AS(0)
        emit_AS(1)
        kv_prime(0)
        for s in range(NTO):
            emit_UB(s)

        P.emit("gpsimd", lambda E: E.nop(), waits=_last(out_dma_toks))
        P.emit("scalar", lambda E: E.nop(), waits=_last(out_dma_toks))
        P.emit("sync", lambda E: E.nop(), waits=_last(out_dma_toks))

        P.replay()
    return nc


_NC_CACHE = {}


def _get_nc(debug=False):
    if debug not in _NC_CACHE:
        _NC_CACHE[debug] = build_nc(debug)
    return _NC_CACHE[debug]


def kernel(L0_attn_x=None, L0_attn_W=None, L0_ffn_Wg=None, L0_ffn_bg=None,
           L0_ffn_We=None, L0_ffn_be=None, _debug=False):
    x = np.ascontiguousarray(np.asarray(L0_attn_x, dtype=np.float32))
    W = np.asarray(L0_attn_W, dtype=np.float32)
    Wg = np.ascontiguousarray(np.asarray(L0_ffn_Wg, dtype=np.float32))
    bgv = np.ascontiguousarray(
        np.asarray(L0_ffn_bg, dtype=np.float32)).reshape(1, NEXP)
    We = np.asarray(L0_ffn_We, dtype=np.float32)
    be = np.ascontiguousarray(np.asarray(L0_ffn_be, dtype=np.float32))

    wk_h = np.ascontiguousarray(W[0:D, :])
    wvt_h = np.ascontiguousarray(W[D:2 * D, :].T)
    wet_h = np.ascontiguousarray(np.transpose(We, (0, 2, 1)))
    ident_h = np.eye(128, dtype=np.float32)

    import ml_dtypes
    r = np.arange(128)[:, None]
    c = np.arange(MASKW)[None, :]
    mask_par = []
    for p in range(2):
        m = np.where(c <= r + 128 * p, 0.0, NEG).astype(ml_dtypes.bfloat16)
        mask_par.append(np.ascontiguousarray(
            np.broadcast_to(m, (NTO, 128, MASKW))))

    nc = _get_nc(_debug)
    in_maps = []
    for core in range(8):
        b, p = core // 2, core % 2
        xb = x[b]
        own = xb.reshape(NTF, 128, D)[[2 * s + p for s in range(NTO)]].reshape(
            NTO * 128, D)
        in_maps.append({
            "xfull": xb,
            "xq": np.ascontiguousarray(own),
            "wk": wk_h, "wvt": wvt_h, "wg": Wg, "bg": bgv,
            "wet": wet_h, "bet": be,
            "masks": mask_par[p], "ident": ident_h,
        })
    res = run_bass_kernel_spmd(nc, in_maps, core_ids=list(range(8)))
    outv = np.empty((B, S, D), dtype=np.float32)
    for core in range(8):
        b, p = core // 2, core % 2
        shard = res.results[core]["out"].reshape(NTO, 128, D)
        for s in range(NTO):
            outv[b, (2 * s + p) * 128:(2 * s + p + 1) * 128, :] = shard[s]
    if _debug:
        return outv, res
    return outv


# version 34 (final)
# speedup vs baseline: 1.1550x; 1.1550x over previous
import numpy as np
from contextlib import ExitStack

import concourse.bass as bass
import concourse.mybir as mybir
from concourse.bass_utils import run_bass_kernel_spmd

F32 = mybir.dt.float32
BF16 = mybir.dt.bfloat16
AF = mybir.ActivationFunctionType
ALU = mybir.AluOpType
AX = mybir.AxisListType.X

B, S, D = 4, 4096, 1024
NJ = 8
NTF = 32
NTO = 16
NEXP = 8
SCW = 2048
MASKW = 256
EPS = 1e-5
NEG = -30000.0
KV_RING = 8
KV_CHUNK = 4
KB = 1024


class Prog:
    ENGINES = ("tensor", "vector", "scalar", "gpsimd", "sync")

    def __init__(self, nc, ctx):
        self.nc = nc
        self.q = {e: [] for e in self.ENGINES}
        self.sem = {}
        self.cnt = {}
        for name in ("dS", "dQ", "dW", "PE", "VE", "AC", "PO"):
            self.sem[name] = ctx.enter_context(nc.semaphore(name))
            self.cnt[name] = 0

    def emit(self, eng, fn, waits=(), inc=None, n=1):
        waits = tuple(w for w in waits if w is not None)
        tok = None
        if inc is not None:
            self.cnt[inc] += n
            tok = (inc, self.cnt[inc])
        self.q[eng].append((waits, fn, inc, n))
        return tok

    def dma(self, eng, out, in_, waits=(), sem="dS"):
        return self.emit(eng, lambda E, o=out, i=in_: E.dma_start(out=o, in_=i),
                         waits, sem, 16)

    def replay(self):
        with self.nc.Block() as block:
            for eng in self.ENGINES:
                queue = self.q[eng]
                sems = self.sem

                def body(E, queue=queue, sems=sems):
                    seen = {}
                    for waits, fn, inc, n in queue:
                        for s, v in waits:
                            if seen.get(s, 0) < v:
                                E.wait_ge(sems[s], v)
                                seen[s] = v
                        ins = fn(E)
                        if inc is not None:
                            ins.then_inc(sems[inc], n)

                getattr(block, eng)(body)


def _last(toks):
    best = {}
    for t in toks:
        if t is None:
            continue
        if isinstance(t, list):
            for tt in t:
                if tt is not None:
                    s, v = tt
                    if best.get(s, 0) < v:
                        best[s] = v
            continue
        s, v = t
        if best.get(s, 0) < v:
            best[s] = v
    return [(s, v) for s, v in best.items()]


def build_nc(debug=False):
    nc = bass.Bass("TRN2", target_bir_lowering=False)

    xfull = nc.dram_tensor("xfull", [S, D], F32, kind="ExternalInput").ap()
    xq = nc.dram_tensor("xq", [NTO * 128, D], F32, kind="ExternalInput").ap()
    wk = nc.dram_tensor("wk", [D, D], F32, kind="ExternalInput").ap()
    wvt = nc.dram_tensor("wvt", [D, D], F32, kind="ExternalInput").ap()
    wg = nc.dram_tensor("wg", [D, NEXP], F32, kind="ExternalInput").ap()
    bg = nc.dram_tensor("bg", [1, NEXP], F32, kind="ExternalInput").ap()
    wet = nc.dram_tensor("wet", [NEXP, D, D], F32, kind="ExternalInput").ap()
    bet = nc.dram_tensor("bet", [NEXP, D], F32, kind="ExternalInput").ap()
    masks = nc.dram_tensor("masks", [NTO, 128, MASKW], BF16, kind="ExternalInput").ap()
    ident = nc.dram_tensor("ident", [128, 128], F32, kind="ExternalInput").ap()
    out = nc.dram_tensor("out", [NTO * 128, D], F32, kind="ExternalOutput").ap()
    scratch_kind = "ExternalOutput" if debug else "Internal"
    xn_hbm = nc.dram_tensor("xn_hbm", [S, D], BF16, kind=scratch_kind).ap()
    g_hbm = nc.dram_tensor("g_hbm", [NTO * 128, D], BF16, kind=scratch_kind).ap()
    if debug:
        dbg_xqfm = nc.dram_tensor("dbg_xqfm", [128, NJ * NTO * 128], BF16,
                                  kind="ExternalOutput").ap()
        dbg_gatew = nc.dram_tensor("dbg_gatew", [128, NTO * NEXP], F32,
                                   kind="ExternalOutput").ap()
        dbg_qt = nc.dram_tensor("dbg_qt", [128, NTO * 1024], BF16,
                                kind="ExternalOutput").ap()
        dbg_z = nc.dram_tensor("dbg_z", [128, NTO * 1024], BF16,
                               kind="ExternalOutput").ap()
        dbg_st = nc.dram_tensor("dbg_st", [128, NTO * 1024], F32,
                                kind="ExternalOutput").ap()
        dbg_ast = nc.dram_tensor("dbg_ast", [128, NTO * 16], F32,
                                 kind="ExternalOutput").ap()

    with ExitStack() as ctx:
        E = ctx.enter_context
        P = Prog(nc, ctx)

        xn_fm = E(nc.sbuf_tensor("xn_fm", [128, NJ, S], BF16))
        xq_fm = E(nc.sbuf_tensor("xq_fm", [128, NJ, NTO * 128], BF16))
        wk_sb = E(nc.sbuf_tensor("wk_sb", [128, NJ, D], BF16))
        wvt_sb = E(nc.sbuf_tensor("wvt_sb", [128, NJ, D], BF16))
        idt = E(nc.sbuf_tensor("idt", [128, 128], BF16))
        wg_sb = E(nc.sbuf_tensor("wg_sb", [128, NJ, NEXP], BF16))
        bg_sb = E(nc.sbuf_tensor("bg_sb", [1, NEXP], BF16))
        be_sb = E(nc.sbuf_tensor("be_sb", [NEXP, D], BF16))
        ones1 = E(nc.sbuf_tensor("ones1", [1, 128], BF16))
        cst = E(nc.sbuf_tensor("cst", [128, 3], F32))
        stat_r = E(nc.sbuf_tensor("stat_r", [128, NTO], F32))
        stat_mr = E(nc.sbuf_tensor("stat_mr", [128, NTO], F32))
        mask_sb = E(nc.sbuf_tensor("mask_sb", [128, MASKW], BF16))
        lnstat_t = E(nc.sbuf_tensor("lnstat_t", [128, 16], F32))
        ast = E(nc.sbuf_tensor("ast", [128, 64], F32))

        ARENA_F32 = 19456
        arena = E(nc.sbuf_tensor("arena", [128, ARENA_F32], F32))
        a16 = arena.bitcast(BF16)

        def f32(off_kb, n):
            o = int(off_kb * KB)
            assert o % 4 == 0 and o + 4 * n <= ARENA_F32 * 4
            return arena[:, o // 4: o // 4 + n]

        def bf16(off_kb, n):
            o = int(off_kb * KB)
            assert o % 2 == 0 and o + 2 * n <= ARENA_F32 * 4
            return a16[:, o // 2: o // 2 + n]

        acc = f32(0, NTO * 512).rearrange("p (t n) -> p t n", t=NTO)
        xin = f32(32, 2 * D).rearrange("p (b n) -> p b n", b=2)
        xn_t = bf16(40, 4 * D).rearrange("p (b n) -> p b n", b=4)
        we_sb = bf16(48, 4 * 4 * 512).rearrange("p (b j n) -> p b j n", b=4, j=4)
        gate_w = f32(64, NTO * NEXP).rearrange("p (t e) -> p t e", t=NTO)
        gate_e = f32(64.5, NTO * NEXP).rearrange("p (t e) -> p t e", t=NTO)
        gmisc = f32(65, NTO * 16).rearrange("p (t c) -> p t c", t=NTO)
        wt_all = bf16(66, NTO * 128)
        g_stage = bf16(70, 2 * 512).rearrange("p (b n) -> p b n", b=2)
        lnstat = lnstat_t[:, :].rearrange("p (b c) -> p b c", b=2)
        scores = f32(0, SCW)
        p_sb = bf16(8, 4 * SCW).rearrange("p (b n) -> p b n", b=4)
        qbuf = bf16(24, D)
        qt_sb = bf16(26, NJ * 128)
        pt_sb = bf16(28, 4 * 256).rearrange("p (b n) -> p b n", b=4)
        z_acc = f32(30, D)
        z_sb = bf16(34, D)
        zt_sb = bf16(36, NJ * 128)
        kv_sb = bf16(38, KV_RING * D).rearrange("p (r n) -> p r n", r=KV_RING)
        xq_in = f32(54, 2 * D).rearrange("p (b n) -> p b n", b=2)
        g_in = bf16(62, 2 * D).rearrange("p (b n) -> p b n", b=2)
        out_sb = f32(66, D)
        xn_ep = f32(70, D)

        ps = [E(nc.psum_tensor(f"ps{i}", [128, 512], F32)) for i in range(8)]
        ps_ln = [ps[3].bitcast(BF16), ps[4].bitcast(BF16)]
        ps_qt = ps[2].bitcast(BF16)
        ps_wt = ps[2].bitcast(BF16)
        ps_pt = ps[7].bitcast(BF16)

        tok_id = P.dma("gpsimd", idt[:, :], ident[:, :], sem="dW")
        tok_wk = P.dma("gpsimd", wk_sb[:, :, :],
                       wk.rearrange("(c p) n -> p c n", p=128), sem="dW")
        tok_wvt = P.dma("gpsimd", wvt_sb[:, :, :],
                        wvt.rearrange("(c p) n -> p c n", p=128), sem="dW")
        tok_wg = P.dma("gpsimd", wg_sb[:, :, :],
                       wg.rearrange("(c p) e -> p c e", p=128), sem="dW")
        tok_bg = P.dma("gpsimd", bg_sb[:, :], bg[:, :], sem="dW")
        tok_be = P.dma("gpsimd", be_sb[:, :], bet[:, :], sem="dW")
        tok_ones = P.emit("gpsimd", lambda E: E.memset(ones1[:, :], 1.0), (), "PO")
        P.emit("gpsimd", lambda E: E.memset(cst[:, 0:1], 1.0 / D))
        P.emit("gpsimd", lambda E: E.memset(cst[:, 1:2], EPS))
        tok_cst = P.emit("gpsimd", lambda E: E.memset(cst[:, 2:3], -0.5), (), "PO")

        xin_free = [[], []]
        xnt_free = [[], [], [], []]
        lnstat_free = [None, None]
        psln_free = [None, None]
        ln_fm_tok = []
        xqfm_tok = [None] * NTO
        stat_tok = [None] * NTO
        gtile_ctr = [0]

        def ln_src(gi):
            if gi >= 32:
                return xq[(gi - 32) * 128:(gi - 31) * 128, :]
            return xfull[gi * 128:(gi + 1) * 128, :]

        sum_toks = {}

        def ln_load_sums(gi):
            par = gi % 2
            par4 = gi % 4
            d_tok = P.dma("sync", xin[:, par, :],
                          ln_src(gi), waits=_last(xin_free[par]), sem="dS")
            sum_toks[gi] = P.emit("vector", lambda E, a=lnstat[:, par, :],
                                  b=xin[:, par, :]:
                                  E.reduce_sum(a[:, 0:1], b, axis=AX, negate=True),
                                  waits=_last([d_tok, lnstat_free[par]]), inc="VE")

        def ln_stats_norm(gi):
            par = gi % 2
            par4 = gi % 4
            st = lnstat[:, par, :]
            xt = xin[:, par, :]
            t_nm = P.emit("scalar", lambda E, a=st:
                          E.mul(a[:, 1:2], a[:, 0:1], 1.0 / D),
                          waits=[sum_toks[gi]], inc="AC")
            t_ssq = P.emit("scalar", lambda E, a=st, b=xt, o=xn_t[:, par4, :]:
                           E.activation(out=o, in_=b, func=AF.Square,
                                        bias=a[:, 1:2], accum_out=a[:, 2:3]),
                           waits=_last([t_nm] + xnt_free[par4]), inc="AC")
            t_std = P.emit("scalar", lambda E, a=st:
                           E.activation(out=a[:, 3:4], in_=a[:, 2:3], func=AF.Sqrt,
                                        scale=cst[:, 0:1], bias=cst[:, 1:2]),
                           waits=[t_ssq, tok_cst], inc="AC")
            t_rstd = P.emit("vector", lambda E, a=st:
                            E.reciprocal(a[:, 4:5], a[:, 3:4]),
                            waits=[t_std], inc="VE")
            t_xn = P.emit("gpsimd", lambda E, a=st, b=xt, o=xn_t[:, par4, :]:
                          E.tensor_scalar(o, b, a[:, 1:2], a[:, 4:5],
                                          ALU.add, ALU.mult),
                          waits=_last([t_rstd, t_ssq]), inc="PO")
            lnstat_free[par] = t_xn
            if gi >= 32:
                sid = gi - 32
                t_sr = P.emit("vector", lambda E, a=st, sid=sid:
                              E.tensor_copy(stat_r[:, sid:sid + 1], a[:, 4:5]),
                              waits=[t_rstd], inc="VE")
                t_smr = P.emit("vector", lambda E, a=st, sid=sid:
                               E.tensor_tensor(stat_mr[:, sid:sid + 1],
                                               a[:, 1:2], a[:, 4:5], ALU.mult),
                               waits=[t_sr], inc="VE")
                stat_tok[sid] = t_smr
            xin_free[par] = [t_ssq, t_xn]
            return t_xn, t_ssq

        def ln_transpose_store(gi, t_xn, t_ssq):
            par4 = gi % 4
            own = gi >= 32
            bpar = gtile_ctr[0] % 2
            tr_tok = None
            for j in range(NJ):
                tr_tok = P.emit("tensor",
                                lambda E, j=j, b=bpar, src=xn_t[:, par4, :]:
                                E.transpose(ps_ln[b][:, j * 128:(j + 1) * 128],
                                            src[:, j * 128:(j + 1) * 128],
                                            idt[:, :]),
                                waits=_last([t_xn, tok_id, psln_free[bpar]]),
                                inc="PE" if j == NJ - 1 else None)
            if own:
                sid = gi - 32
                dest = xq_fm[:, :, sid * 128:(sid + 1) * 128]
            else:
                dest = xn_fm[:, :, gi * 128:(gi + 1) * 128]
            cp_eng = "vector" if (gtile_ctr[0] % 2 == 0) else "scalar"

            def _cp(E, d=dest, b=bpar):
                src = ps_ln[b][:, :].rearrange("p (j t) -> p j t", j=NJ)
                return (E.tensor_copy(d, src) if hasattr(E, "tensor_copy")
                        else E.copy(d, src))
            t_cp = P.emit(cp_eng, _cp, waits=[tr_tok],
                          inc="VE" if cp_eng == "vector" else "AC")
            psln_free[bpar] = t_cp
            if own:
                xqfm_tok[gi - 32] = t_cp
            else:
                ln_fm_tok.append(t_cp)
            gtile_ctr[0] += 1
            if not own:
                s_tok = P.dma("scalar",
                              xn_hbm[gi * 128:(gi + 1) * 128, :],
                              xn_t[:, par4, :], waits=[t_xn], sem="dQ")
                xnt_free[par4] = [s_tok, tr_tok]
            else:
                xnt_free[par4] = [tr_tok]

        pairs = [(h, e) for h in range(2) for e in range(NEXP)]
        tok_we = [None] * 4
        we_free = [None] * 4

        def emit_we_pair(i):
            h, e = pairs[i]
            for jh in range(2):
                buf = (2 * i + jh) % 4
                src = wet[e, jh * 512:(jh + 1) * 512,
                          h * 512:(h + 1) * 512].rearrange(
                    "(c p) n -> p c n", p=128)
                tok_we[buf] = P.dma("gpsimd", we_sb[:, buf, :, :], src,
                                    waits=_last([we_free[buf]]), sem="dW")

        emit_we_pair(0)
        emit_we_pair(1)

        ln_load_sums(32)
        for gi in range(32, 48):
            if gi + 1 < 48:
                ln_load_sums(gi + 1)
            t_xn, t_ssq = ln_stats_norm(gi)
            ln_transpose_store(gi, t_xn, t_ssq)

        gl = ps[2][:, 0:NTO * NEXP].rearrange("p (t e) -> p t e", t=NTO)
        mm_tok = [None] * NTO
        for t in range(NTO):
            for j in range(NJ):
                P.emit("tensor", lambda E, j=j, t=t:
                       E.matmul(gl[:, t, :],
                                xq_fm[:, j, t * 128:(t + 1) * 128],
                                wg_sb[:, j, :], start=(j == 0), stop=False),
                       waits=_last([xqfm_tok[t], tok_wg, tok_bg, tok_ones, tok_id]))
            mm_tok[t] = P.emit("tensor", lambda E, t=t:
                               E.matmul(gl[:, t, :], ones1[0:1, :], bg_sb[0:1, :],
                                        start=False, stop=True), inc="PE")
        tg = [None] * NTO
        for t in range(NTO):
            tg[t] = P.emit("vector", lambda E, t=t:
                           E.reduce_max(gmisc[:, t, 0:1], gl[:, t, :], axis=AX,
                                        negate=True),
                           waits=[mm_tok[NTO - 1]], inc="VE")
        te = [None] * NTO
        for t in range(NTO):
            te[t] = P.emit("scalar", lambda E, t=t:
                           E.activation(out=gate_e[:, t, :], in_=gl[:, t, :],
                                        func=AF.Exp, bias=gmisc[:, t, 0:1]),
                           waits=[tg[t]], inc="AC")
        t1 = [P.emit("vector", lambda E, t=t:
                     E.reduce_max(gmisc[:, t, 1:2], gate_e[:, t, :], axis=AX),
                     waits=[te[t]], inc="VE") for t in range(NTO)]
        t2 = [P.emit("vector", lambda E, t=t:
                     E.tensor_scalar(gmisc[:, t, 8:16], gate_e[:, t, :],
                                     gmisc[:, t, 1:2], 0.0, ALU.is_ge, ALU.add),
                     waits=[t1[t]], inc="VE") for t in range(NTO)]
        t3 = [P.emit("vector", lambda E, t=t:
                     E.scalar_tensor_tensor(gmisc[:, t, 8:16], gmisc[:, t, 8:16],
                                            -1e4, gate_e[:, t, :],
                                            ALU.mult, ALU.add),
                     waits=[t2[t]], inc="VE") for t in range(NTO)]
        t4 = [P.emit("vector", lambda E, t=t:
                     E.reduce_max(gmisc[:, t, 2:3], gmisc[:, t, 8:16], axis=AX),
                     waits=[t3[t]], inc="VE") for t in range(NTO)]
        t5 = [P.emit("vector", lambda E, t=t:
                     E.tensor_tensor(gmisc[:, t, 3:4], gmisc[:, t, 1:2],
                                     gmisc[:, t, 2:3], ALU.add),
                     waits=[t4[t]], inc="VE") for t in range(NTO)]
        t6 = [P.emit("vector", lambda E, t=t:
                     E.reciprocal(gmisc[:, t, 4:5], gmisc[:, t, 3:4]),
                     waits=[t5[t]], inc="VE") for t in range(NTO)]
        t7 = [P.emit("vector", lambda E, t=t:
                     E.tensor_scalar(gmisc[:, t, 8:16], gate_e[:, t, :],
                                     gmisc[:, t, 2:3], 0.0, ALU.is_ge, ALU.add),
                     waits=[t4[t]], inc="VE") for t in range(NTO)]
        t8 = [P.emit("vector", lambda E, t=t:
                     E.tensor_tensor(gmisc[:, t, 8:16], gmisc[:, t, 8:16],
                                     gate_e[:, t, :], ALU.mult),
                     waits=[t7[t]], inc="VE") for t in range(NTO)]
        gatew_tok = [None] * NTO
        for t in range(NTO):
            gatew_tok[t] = P.emit("vector", lambda E, t=t:
                                  E.tensor_scalar(gate_w[:, t, :],
                                                  gmisc[:, t, 8:16],
                                                  gmisc[:, t, 4:5],
                                                  0.0, ALU.mult, ALU.add),
                                  waits=[t8[t], t6[t]], inc="VE")
        wtall_tok = [None] * NTO
        p2_free = te[NTO - 1]

        def emit_wt_all():
            nonlocal p2_free
            tb16 = [P.emit("vector", lambda E, t=t:
                           E.tensor_copy(g_stage[:, 0, t * 8:(t + 1) * 8],
                                         gate_w[:, t, :]),
                           waits=[gatew_tok[t]], inc="VE") for t in range(NTO)]
            for t in range(NTO):
                t11 = P.emit("tensor", lambda E, t=t:
                             E.transpose(ps_wt[0:8, 256:384],
                                         g_stage[:, 0, t * 8:(t + 1) * 8],
                                         idt[:, :]),
                             waits=_last([tb16[t], p2_free] + te), inc="PE")
                t12 = P.emit("scalar", lambda E, t=t:
                             E.copy(wt_all[0:8, t * 128:(t + 1) * 128],
                                    ps_wt[0:8, 256:384]),
                             waits=[t11], inc="AC")
                wtall_tok[t] = t12
                p2_free = t12

        EXP_BANKS = [0, 1, 5, 6]
        psy_free = [None, None, None, None]
        acc_tok = [None] * NTO
        acc_free = [None] * NTO
        g_stage_free = [None, None]
        g_store_tok = []
        last_bias_mm = None

        ln_pending = []
        ln_load_sums(0)
        ln_load_sums(1)

        def ln_full_step(k):
            if k < 32:
                ln_pending.append((k, ln_stats_norm(k)))

        def ln_full_flush(upto):
            while ln_pending and ln_pending[0][0] <= upto:
                k, (t_xn, t_ssq) = ln_pending.pop(0)
                ln_transpose_store(k, t_xn, t_ssq)

        for i, (h, e) in enumerate(pairs):
            if i + 1 < len(pairs) and i >= 1:
                emit_we_pair(i + 1)
            if i == 1:
                emit_wt_all()
            ln_full_step(2 * i)
            ln_full_step(2 * i + 1)
            last_mm = None
            for t in range(NTO):
                if t == 8:
                    for k in (2 * i + 2, 2 * i + 3):
                        if k < 32:
                            ln_load_sums(k)
                    ln_full_flush(2 * i + 1)
                bpar = t % 4
                bank = EXP_BANKS[bpar]
                mm = None
                for j in range(NJ):
                    jh, jr = (0, j) if j < 4 else (1, j - 4)
                    buf = (2 * i + jh) % 4
                    mm = P.emit("tensor", lambda E, j=j, t=t, b=bank, bu=buf, jr=jr:
                                E.matmul(ps[b][:, :],
                                         xq_fm[:, j, t * 128:(t + 1) * 128],
                                         we_sb[:, bu, jr, :],
                                         start=(j == 0), stop=(j == NJ - 1)),
                                waits=_last([tok_we[buf], psy_free[bpar],
                                             xqfm_tok[t]]),
                                inc="PE" if j == NJ - 1 else None)
                last_mm = mm
                if e == 0:
                    tv = P.emit("vector", lambda E, t=t, b=bank:
                                E.tensor_scalar(acc[:, t, :], ps[b][:, :],
                                                gate_w[:, t, 0:1], 0.0,
                                                ALU.mult, ALU.add),
                                waits=_last([mm, gatew_tok[t], acc_free[t]]),
                                inc="VE")
                else:
                    tv = P.emit("vector", lambda E, t=t, b=bank, e=e:
                                E.scalar_tensor_tensor(acc[:, t, :], ps[b][:, :],
                                                       gate_w[:, t, e:e + 1],
                                                       acc[:, t, :],
                                                       ALU.mult, ALU.add),
                                waits=_last([mm, acc_tok[t]]), inc="VE")
                acc_tok[t] = tv
                psy_free[bpar] = tv
            we_free[(2 * i) % 4] = last_mm
            we_free[(2 * i + 1) % 4] = last_mm
            if e == NEXP - 1:
                for t in range(NTO):
                    mmb = P.emit("tensor", lambda E, t=t, h=h:
                                 E.matmul(ps[2][:, :],
                                          wt_all[0:8, t * 128:(t + 1) * 128],
                                          be_sb[0:8, h * 512:(h + 1) * 512],
                                          start=True, stop=True),
                                 waits=_last([wtall_tok[t], tok_be, p2_free]),
                                 inc="PE")
                    last_bias_mm = mmb
                    tb = P.emit("vector", lambda E, t=t:
                                E.tensor_tensor(acc[:, t, :], acc[:, t, :],
                                                ps[2][:, :], ALU.add),
                                waits=_last([mmb, acc_tok[t]]), inc="VE")
                    p2_free = tb
                    acc_tok[t] = tb
                    gpar = t % 2
                    tsil = P.emit("scalar", lambda E, t=t, b=gpar:
                                  E.activation(out=g_stage[:, b, :],
                                               in_=acc[:, t, :], func=AF.Silu),
                                  waits=_last([tb, g_stage_free[gpar]]), inc="AC")
                    gd = P.dma("gpsimd",
                               g_hbm[t * 128:(t + 1) * 128, h * 512:(h + 1) * 512],
                               g_stage[:, gpar, :], waits=[tsil], sem="dW")
                    g_stage_free[gpar] = gd
                    g_store_tok.append(gd)
                    acc_free[t] = tsil
        ln_full_flush(31)

        xn_stores_done = ("dQ", P.cnt["dQ"])
        ln_done = _last(ln_fm_tok + xqfm_tok + [psln_free[0], psln_free[1]])
        ln_xnt_done = _last(xnt_free[0] + xnt_free[1])

        if debug:
            P.dma("sync", dbg_xqfm[:, :].rearrange("p (j n) -> p j n", j=NJ),
                  xq_fm[:, :, :], waits=ln_done, sem="dS")

        moe_done = _last(acc_tok + g_store_tok + [p2_free, psy_free[0], psy_free[1],
                                                  last_bias_mm] + acc_free +
                         ln_done + ln_xnt_done + [xn_stores_done])

        if debug:
            P.dma("sync", dbg_gatew[:, :].rearrange("p (t e) -> p t e", t=NTO),
                  gate_w[:, :, :], waits=moe_done, sem="dS")

        def nchunks(s):
            return (256 * (s + 1) + SCW - 1) // SCW

        mask_free = [None]
        mask_tok = [None] * NTO
        xqin_free = [None, None]
        xqin_tok = [None] * NTO
        gin_free = [None, None]
        gin_tok = [None] * NTO
        out_free = [None]
        out_dma_toks = []

        kv_ring_free = [None] * KV_RING
        kv_cnt = [0]

        p01_free = [psy_free[0], psy_free[1]]
        p34_free = [psln_free[0], psln_free[1]]
        p56_free = [_last([psy_free[2], psy_free[3]])[0]]
        p2qt_free = [p2_free]
        p7_free = [None]
        ppt_free = [None, None]
        ptsb_free = [None] * 4
        qt_free = [None]
        qbuf_free = [None]

        sc_free = [None]
        p_free = [None] * 4
        pexp_tok = {}
        gsc = [0]
        slot_state = {}

        def prefetch_mask(s):
            barrier = list(moe_done) if s < 2 else []
            mask_tok[s] = P.dma("scalar", mask_sb[:, :], masks[s, :, :],
                                waits=_last([mask_free[0]] + barrier), sem="dQ")

        def prefetch_slot(s):
            par = s % 2
            barrier = list(moe_done) if s < 2 else []
            xqin_tok[s] = P.dma("gpsimd", xq_in[:, par, :],
                                xq[s * 128:(s + 1) * 128, :],
                                waits=_last([xqin_free[par]] + barrier), sem="dW")
            gin_tok[s] = P.dma("gpsimd", g_in[:, par, :],
                               g_hbm[s * 128:(s + 1) * 128, :],
                               waits=_last([gin_free[par], g_store_tok[s],
                                            g_store_tok[NTO + s]] + barrier),
                               sem="dW")

        def emit_A1(s):
            toks = []
            for half in range(2):
                mm = None
                for j in range(NJ):
                    mm = P.emit("tensor", lambda E, j=j, h=half, s=s:
                                E.matmul(ps[h][:, :],
                                         xq_fm[:, j, s * 128:(s + 1) * 128],
                                         wk_sb[:, j, h * 512:(h + 1) * 512],
                                         start=(j == 0), stop=(j == NJ - 1)),
                                waits=_last([tok_wk, xqfm_tok[s], p01_free[half]]),
                                inc="PE" if j == NJ - 1 else None)
                toks.append(mm)
            ev = []
            for half in range(2):
                e = P.emit("scalar", lambda E, h=half:
                           E.copy(qbuf[:, h * 512:(h + 1) * 512], ps[h][:, :]),
                           waits=_last([toks[half], qbuf_free[0]]), inc="AC")
                ev.append(e)
            p01_free[0] = ev[0]
            p01_free[1] = ev[1]
            return ev

        def emit_A2(s, ev):
            trt = None
            for j in range(NJ):
                trt = P.emit("tensor", lambda E, j=j:
                             E.transpose(ps_qt[:, j * 128:(j + 1) * 128],
                                         qbuf[:, j * 128:(j + 1) * 128], idt[:, :]),
                             waits=_last(ev + [p2qt_free[0]]),
                             inc="PE" if j == NJ - 1 else None)
            qbuf_free[0] = trt
            cq = P.emit("vector", lambda E:
                        E.tensor_copy(qt_sb[:, :], ps_qt[:, :]),
                        waits=_last([trt, qt_free[0]]), inc="VE")
            p2qt_free[0] = cq
            if debug:
                P.dma("sync", dbg_qt[:, s * 1024:(s + 1) * 1024], qt_sb[:, :],
                      waits=[cq], sem="dS")
            return cq

        def emit_S(s, c, cq_tok):
            W = min(SCW, 256 * (s + 1) - c * SCW)
            nsub = (W + 511) // 512
            ab = (s % 4) * 16
            last_chunk = (c == nchunks(s) - 1)
            evs = []
            submax = []
            last_mm = None
            for w in range(nsub):
                n = min(512, W - w * 512)
                bank = 3 + (w % 2)
                mm = None
                for j in range(NJ):
                    mm = P.emit("tensor",
                                lambda E, j=j, w=w, n=n, bank=bank, s=s, c=c:
                                E.matmul(ps[bank][:, 0:n],
                                         qt_sb[:, j * 128:(j + 1) * 128],
                                         xn_fm[:, j, c * SCW + w * 512:
                                               c * SCW + w * 512 + n],
                                         start=(j == 0), stop=(j == NJ - 1)),
                                waits=_last([cq_tok, p34_free[bank - 3]] + ln_done),
                                inc="PE" if j == NJ - 1 else None)
                last_mm = mm
                evac = P.emit("scalar", lambda E, w=w, n=n, bank=bank:
                              E.copy(scores[:, w * 512:w * 512 + n],
                                     ps[bank][:, 0:n]),
                              waits=_last([mm, sc_free[0]]), inc="AC")
                p34_free[bank - 3] = evac
                evs.append(evac)
                mwait = [evac]
                if last_chunk and w == nsub - 1:
                    mtok = P.emit("vector", lambda E, W=W, s=s:
                                  E.tensor_tensor(scores[:, W - MASKW:W],
                                                  scores[:, W - MASKW:W],
                                                  mask_sb[:, :], ALU.add),
                                  waits=_last([evac, mask_tok[s], mask_free[0]]),
                                  inc="VE")
                    mask_free[0] = mtok
                    mwait = [mtok]
                sm_t = P.emit("vector", lambda E, w=w, n=n, ab=ab:
                              E.reduce_max(ast[:, ab + 9 + w:ab + 10 + w],
                                           scores[:, w * 512:w * 512 + n],
                                           axis=AX, negate=True),
                              waits=_last(mwait), inc="VE")
                submax.append(sm_t)
            return evs, submax, last_mm

        def emit_softmax(s, c, evs, submax):
            W = min(SCW, 256 * (s + 1) - c * SCW)
            nsub = (W + 511) // 512
            par4 = gsc[0] % 4
            ab = (s % 4) * 16
            cm = submax[0]
            if nsub == 1:
                cm = P.emit("vector", lambda E, ab=ab, c=c:
                            E.tensor_copy(ast[:, ab + c:ab + c + 1],
                                          ast[:, ab + 9:ab + 10]),
                            waits=[submax[0]], inc="VE")
            else:
                for w in range(1, nsub):
                    src0 = (9 if w == 1 else c)
                    cm = P.emit("vector", lambda E, ab=ab, c=c, w=w, src0=src0:
                                E.tensor_tensor(ast[:, ab + c:ab + c + 1],
                                                ast[:, ab + src0:ab + src0 + 1],
                                                ast[:, ab + 9 + w:ab + 10 + w],
                                                ALU.min),
                                waits=_last([submax[w], cm]), inc="VE")
            if c == 0:
                bias_tok, bias_col = cm, ast[:, ab + 0:ab + 1]
            else:
                nm_t = P.emit("vector", lambda E, ab=ab:
                              E.tensor_tensor(ast[:, ab + 4:ab + 5],
                                              ast[:, ab + 0:ab + 1],
                                              ast[:, ab + 1:ab + 2], ALU.min),
                              waits=[cm], inc="VE")
                slot_state[s]["nm_tok"] = nm_t
                d0 = P.emit("vector", lambda E, ab=ab:
                            E.tensor_tensor(ast[:, ab + 5:ab + 6],
                                            ast[:, ab + 4:ab + 5],
                                            ast[:, ab + 0:ab + 1], ALU.subtract),
                            waits=[nm_t], inc="VE")
                a0 = P.emit("scalar", lambda E, ab=ab:
                            E.activation(out=ast[:, ab + 6:ab + 7],
                                         in_=ast[:, ab + 5:ab + 6], func=AF.Exp),
                            waits=[d0], inc="AC")
                slot_state[s]["a0_tok"] = a0
                bias_tok, bias_col = nm_t, ast[:, ab + 4:ab + 5]
            ex = P.emit("scalar", lambda E, W=W, b=bias_col, c=c, ab=ab, par4=par4:
                        E.activation(out=p_sb[:, par4, 0:W], in_=scores[:, 0:W],
                                     func=AF.Exp, bias=b,
                                     accum_out=ast[:, ab + 2 + c:ab + 3 + c]),
                        waits=_last([bias_tok, p_free[par4]]), inc="AC")
            pexp_tok[(s, c)] = ex
            sc_free[0] = ex
            return ex

        kv_map = {}
        kv_primed = {}

        def kv_dma(s, c, kt0, n):
            pos = kv_cnt[0] % KV_RING
            n = min(n, KV_RING - pos)
            gkt0 = c * (SCW // 128) + kt0
            rows = xn_hbm[gkt0 * 128:(gkt0 + n) * 128, :]
            queue = "sync" if (kv_cnt[0] // KV_CHUNK) % 2 == 0 else "scalar"
            waits = _last([kv_ring_free[pos + i] for i in range(n)] +
                          [xn_stores_done] +
                          (list(moe_done) if kv_cnt[0] < KV_RING else []))
            tok = P.dma(queue, kv_sb[:, pos:pos + n, :],
                        rows.rearrange("(c p) n -> p c n", p=128),
                        waits=waits, sem="dS" if queue == "sync" else "dQ")
            for i in range(n):
                kv_map[(s, c, kt0 + i)] = (tok, pos + i)
            kv_cnt[0] += n
            return n

        def kv_prime(s):
            nkt = min(2 * (s + 1), SCW // 128)
            loaded = 0
            while loaded < min(nkt, KV_RING):
                loaded += kv_dma(s, 0, loaded,
                                 min(KV_CHUNK, nkt - loaded, KV_RING - loaded))
            kv_primed[s] = loaded

        def emit_Z(s, c):
            W = min(SCW, 256 * (s + 1) - c * SCW)
            nkt = W // 128
            par4 = slot_state[s]["sc_par"][c]
            ex = pexp_tok[(s, c)]
            ngrp = (nkt + 1) // 2
            cp_tok = [None] * ngrp
            pt_banks = [ps_pt, ps_qt]

            def z_group(g):
                kt0 = g * 2
                n = min(2, nkt - kt0)
                last = None
                for i in range(n):
                    kt = kt0 + i
                    dtok, ring = kv_map[(s, c, kt)]
                    for half in range(2):
                        last = P.emit(
                            "tensor",
                            lambda E, i=i, h=half, r=ring, g=g, kt=kt, nkt=nkt:
                            E.matmul(ps[5 + h][:, :],
                                     pt_sb[:, g % 4, i * 128:(i + 1) * 128],
                                     kv_sb[:, r, h * 512:(h + 1) * 512],
                                     start=(kt == 0), stop=(kt == nkt - 1)),
                            waits=_last([cp_tok[g], dtok] +
                                        ([p56_free[0]] if kt == 0 else [])),
                            inc="PE" if half == 1 else None)
                    kv_ring_free[ring] = last
                return last

            if c == 0:
                kv_loaded = kv_primed.get(s, 0)
            else:
                kv_loaded = 0
            while kv_loaded < min(nkt, KV_RING):
                kv_loaded += kv_dma(s, c, kv_loaded,
                                    min(KV_CHUNK, nkt - kv_loaded,
                                        KV_RING - kv_loaded))
            last_z = None

            def tr_group(g):
                kt0 = g * 2
                n = min(2, nkt - kt0)
                bankt = pt_banks[g % 2]
                tr = None
                for i in range(n):
                    kt = kt0 + i
                    tr = P.emit("tensor", lambda E, i=i, kt=kt, par4=par4, bk=bankt:
                                E.transpose(bk[:, i * 128:(i + 1) * 128],
                                            p_sb[:, par4, kt * 128:(kt + 1) * 128],
                                            idt[:, :]),
                                waits=_last([ex, ppt_free[g % 2]]),
                                inc="PE" if i == n - 1 else None)
                cp = P.emit("vector", lambda E, g=g, n=n, bk=bankt:
                            E.tensor_copy(pt_sb[:, g % 4, 0:n * 128],
                                          bk[:, 0:n * 128]),
                            waits=_last([tr, ptsb_free[g % 4]]), inc="VE")
                cp_tok[g] = cp
                ppt_free[g % 2] = cp
                if g % 2 == 0:
                    p7_free[0] = cp

            for g in range(min(2, ngrp)):
                tr_group(g)
            for g in range(ngrp):
                if g + 2 < ngrp:
                    tr_group(g + 2)
                z = z_group(g)
                ptsb_free[g % 4] = z
                last_z = z
                if g >= 1 and kv_loaded < nkt:
                    kv_loaded += kv_dma(s, c, kv_loaded, min(2, nkt - kv_loaded))
            p_free[par4] = last_z
            return last_z

        def emit_zfin(s):
            st = slot_state[s]
            ab = (s % 4) * 16
            nch = nchunks(s)
            if nch == 1:
                rs = P.emit("vector", lambda E, ab=ab:
                            E.reciprocal(ast[:, ab + 7:ab + 8],
                                         ast[:, ab + 2:ab + 3]),
                            waits=[pexp_tok[(s, 0)]], inc="VE")
                zs = []
                for half in range(2):
                    zz = P.emit("vector", lambda E, h=half, ab=ab:
                                E.tensor_scalar(z_sb[:, h * 512:(h + 1) * 512],
                                                ps[5 + h][:, :],
                                                ast[:, ab + 7:ab + 8], 0.0,
                                                ALU.mult, ALU.add),
                                waits=_last([st["z_toks"][-1], rs,
                                             st["zsb_free"]]),
                                inc="VE")
                    zs.append(zz)
                z_ready = zs
            else:
                a0 = st["a0_tok"]
                zts = []
                for half in range(2):
                    zt = P.emit("vector", lambda E, h=half, ab=ab:
                                E.scalar_tensor_tensor(
                                    z_acc[:, h * 512:(h + 1) * 512],
                                    z_acc[:, h * 512:(h + 1) * 512],
                                    ast[:, ab + 6:ab + 7], ps[5 + h][:, :],
                                    ALU.mult, ALU.add),
                                waits=_last([a0, st["z_toks"][-1],
                                             st["zacc_tok"]]),
                                inc="VE")
                    zts.append(zt)
                sm = P.emit("vector", lambda E, ab=ab:
                            E.scalar_tensor_tensor(ast[:, ab + 8:ab + 9],
                                                   ast[:, ab + 2:ab + 3],
                                                   ast[:, ab + 6:ab + 7],
                                                   ast[:, ab + 3:ab + 4],
                                                   ALU.mult, ALU.add),
                            waits=_last([a0, pexp_tok[(s, 1)]]), inc="VE")
                rs = P.emit("vector", lambda E, ab=ab:
                            E.reciprocal(ast[:, ab + 7:ab + 8],
                                         ast[:, ab + 8:ab + 9]),
                            waits=[sm], inc="VE")
                zz = P.emit("vector", lambda E, ab=ab:
                            E.tensor_scalar(z_sb[:, :], z_acc[:, :],
                                            ast[:, ab + 7:ab + 8], 0.0,
                                            ALU.mult, ALU.add),
                            waits=_last(zts + [rs, st["zsb_free"]]), inc="VE")
                z_ready = [zz]
            st["z_ready"] = z_ready

        def emit_FIN_ZTAV(s):
            st = slot_state[s]
            ab = (s % 4) * 16
            z_ready = st["z_ready"]
            trt = None
            for j in range(NJ):
                trt = P.emit("tensor", lambda E, j=j:
                             E.transpose(ps_pt[:, j * 128:(j + 1) * 128],
                                         z_sb[:, j * 128:(j + 1) * 128], idt[:, :]),
                             waits=_last(z_ready + [p7_free[0]]),
                             inc="PE" if j == NJ - 1 else None)
            czt = P.emit("vector", lambda E:
                         E.tensor_copy(zt_sb[:, :], ps_pt[:, :]),
                         waits=_last([trt, st["zt_free"]]), inc="VE")
            p7_free[0] = czt
            av = []
            for half in range(2):
                mm = None
                for j in range(NJ):
                    mm = P.emit("tensor", lambda E, j=j, h=half:
                                E.matmul(ps[5 + h][:, :],
                                         zt_sb[:, j * 128:(j + 1) * 128],
                                         wvt_sb[:, j, h * 512:(h + 1) * 512],
                                         start=(j == 0), stop=(j == NJ - 1)),
                                waits=_last([czt, tok_wvt] + z_ready),
                                inc="PE" if j == NJ - 1 else None)
                av.append(mm)
            if debug:
                P.dma("sync", dbg_z[:, s * 1024:(s + 1) * 1024], z_sb[:, :],
                      waits=_last(z_ready), sem="dS")
                P.dma("sync", dbg_ast[:, s * 16:(s + 1) * 16],
                      ast[:, ab:ab + 16], waits=_last(z_ready), sem="dS")
            st["av_toks"] = av

        def emit_EP(s):
            st = slot_state[s]
            av = st["av_toks"]
            xe = P.emit("scalar", lambda E, s=s:
                        E.activation(out=xn_ep[:, :], in_=xq_in[:, s % 2, :],
                                     func=AF.Identity, scale=stat_r[:, s:s + 1],
                                     bias=stat_mr[:, s:s + 1]),
                        waits=_last([xqin_tok[s], stat_tok[s], st["xnep_free"]]),
                        inc="AC")
            sts = []
            for half in range(2):
                t = P.emit("vector", lambda E, h=half:
                           E.tensor_tensor(xn_ep[:, h * 512:(h + 1) * 512],
                                           ps[5 + h][:, :],
                                           xn_ep[:, h * 512:(h + 1) * 512], ALU.add),
                           waits=_last([av[h], xe]), inc="VE")
                sts.append(t)
            dbg_tok = None
            if debug:
                dbg_tok = P.dma("sync", dbg_st[:, s * 1024:(s + 1) * 1024],
                                xn_ep[:, :], waits=_last(sts), sem="dS")
            sg = P.emit("vector", lambda E, s=s:
                        E.tensor_tensor(xn_ep[:, :], xn_ep[:, :],
                                        g_in[:, s % 2, :], ALU.mult),
                        waits=_last(sts + [gin_tok[s], dbg_tok]), inc="VE")
            ob = P.emit("vector", lambda E, s=s:
                        E.tensor_tensor(out_sb[:, :], xn_ep[:, :],
                                        xq_in[:, s % 2, :], ALU.add),
                        waits=_last([sg, out_free[0]]), inc="VE")
            od = P.dma("gpsimd", out[s * 128:(s + 1) * 128, :], out_sb[:, :],
                       waits=[ob], sem="dW")
            out_free[0] = od
            xqin_free[s % 2] = ob
            gin_free[s % 2] = ob
            return sts, sg, ob, od

        fin_prev = {"zsb": None, "zt": None, "xnep": None}

        def emit_AS(s):
            prefetch_mask(s)
            prefetch_slot(s)
            slot_state[s] = {"sc_par": {}, "z_toks": [], "S_last_mm": None}
            ev = emit_A1(s)
            cq = emit_A2(s, ev)
            slot_state[s]["cq"] = cq
            emit_Ssoft(s)

        def emit_Ssoft(s):
            cq = slot_state[s]["cq"]
            for c in range(nchunks(s)):
                slot_state[s]["sc_par"][c] = gsc[0] % 4
                evs, submax, last_mm = emit_S(s, c, cq)
                slot_state[s]["S_last_mm"] = last_mm
                emit_softmax(s, c, evs, submax)
                gsc[0] += 1
            qt_free[0] = slot_state[s]["S_last_mm"]

        def emit_UB(s):
            st = slot_state[s]
            st["zsb_free"] = fin_prev["zsb"]
            st["zt_free"] = fin_prev["zt"]
            st["xnep_free"] = fin_prev["xnep"]
            ppt_free[0] = p7_free[0]
            ppt_free[1] = p2qt_free[0]
            nch = nchunks(s)
            for c in range(nch):
                z = emit_Z(s, c)
                st["z_toks"].append(z)
                if nch > 1 and c == 0:
                    zes = []
                    for half in range(2):
                        ze = P.emit("vector", lambda E, h=half:
                                    E.tensor_copy(z_acc[:, h * 512:(h + 1) * 512],
                                                  ps[5 + h][:, :]),
                                    waits=[z], inc="VE")
                        zes.append(ze)
                    st["zacc_tok"] = zes[-1]
                    p56_free[0] = zes[-1]
            emit_zfin(s)
            p2qt_free[0] = ppt_free[1]
            if s + 2 < NTO:
                slot_state[s + 2] = {"sc_par": {}, "z_toks": [],
                                     "S_last_mm": None}
                ev2 = emit_A1(s + 2)
            emit_FIN_ZTAV(s)
            if s + 2 < NTO:
                cq = emit_A2(s + 2, ev2)
                slot_state[s + 2]["cq"] = cq
                prefetch_mask(s + 2)
            sts, sg, ob, od = emit_EP(s)
            p56_free[0] = _last(sts)[0]
            if s + 2 < NTO:
                emit_Ssoft(s + 2)
                prefetch_slot(s + 2)
            if s + 1 < NTO:
                kv_prime(s + 1)
            out_dma_toks.append(od)
            fin_prev["zsb"] = [("PE", P.cnt["PE"])] + \
                ([("dS", P.cnt["dS"])] if debug else [])
            fin_prev["zt"] = ("PE", P.cnt["PE"])
            fin_prev["xnep"] = ob

        emit_AS(0)
        emit_AS(1)
        kv_prime(0)
        for s in range(NTO):
            emit_UB(s)

        P.emit("gpsimd", lambda E: E.nop(), waits=_last(out_dma_toks))
        P.emit("scalar", lambda E: E.nop(), waits=_last(out_dma_toks))
        P.emit("sync", lambda E: E.nop(), waits=_last(out_dma_toks))

        P.replay()
    return nc


_NC_CACHE = {}


def _get_nc(debug=False):
    if debug not in _NC_CACHE:
        _NC_CACHE[debug] = build_nc(debug)
    return _NC_CACHE[debug]


def kernel(L0_attn_x=None, L0_attn_W=None, L0_ffn_Wg=None, L0_ffn_bg=None,
           L0_ffn_We=None, L0_ffn_be=None, _debug=False):
    x = np.ascontiguousarray(np.asarray(L0_attn_x, dtype=np.float32))
    W = np.asarray(L0_attn_W, dtype=np.float32)
    Wg = np.ascontiguousarray(np.asarray(L0_ffn_Wg, dtype=np.float32))
    bgv = np.ascontiguousarray(
        np.asarray(L0_ffn_bg, dtype=np.float32)).reshape(1, NEXP)
    We = np.asarray(L0_ffn_We, dtype=np.float32)
    be = np.ascontiguousarray(np.asarray(L0_ffn_be, dtype=np.float32))

    wk_h = np.ascontiguousarray(W[0:D, :])
    wvt_h = np.ascontiguousarray(W[D:2 * D, :].T)
    wet_h = np.ascontiguousarray(np.transpose(We, (0, 2, 1)))
    ident_h = np.eye(128, dtype=np.float32)

    import ml_dtypes
    r = np.arange(128)[:, None]
    c = np.arange(MASKW)[None, :]
    mask_par = []
    for p in range(2):
        m = np.where(c <= r + 128 * p, 0.0, NEG).astype(ml_dtypes.bfloat16)
        mask_par.append(np.ascontiguousarray(
            np.broadcast_to(m, (NTO, 128, MASKW))))

    nc = _get_nc(_debug)
    in_maps = []
    for core in range(8):
        b, p = core // 2, core % 2
        xb = x[b]
        own = xb.reshape(NTF, 128, D)[[2 * s + p for s in range(NTO)]].reshape(
            NTO * 128, D)
        in_maps.append({
            "xfull": xb,
            "xq": np.ascontiguousarray(own),
            "wk": wk_h, "wvt": wvt_h, "wg": Wg, "bg": bgv,
            "wet": wet_h, "bet": be,
            "masks": mask_par[p], "ident": ident_h,
        })
    res = run_bass_kernel_spmd(nc, in_maps, core_ids=list(range(8)))
    outv = np.empty((B, S, D), dtype=np.float32)
    for core in range(8):
        b, p = core // 2, core % 2
        shard = res.results[core]["out"].reshape(NTO, 128, D)
        for s in range(NTO):
            outv[b, (2 * s + p) * 128:(2 * s + p + 1) * 128, :] = shard[s]
    if _debug:
        return outv, res
    return outv


# version 35
# speedup vs baseline: 1.1596x; 1.0040x over previous
import numpy as np
from contextlib import ExitStack

import concourse.bass as bass
import concourse.mybir as mybir
from concourse.bass_utils import run_bass_kernel_spmd

F32 = mybir.dt.float32
BF16 = mybir.dt.bfloat16
AF = mybir.ActivationFunctionType
ALU = mybir.AluOpType
AX = mybir.AxisListType.X

B, S, D = 4, 4096, 1024
NJ = 8
NTF = 32
NTO = 16
NEXP = 8
SCW = 2048
MASKW = 256
EPS = 1e-5
NEG = -30000.0
KV_RING = 8
KV_CHUNK = 4
KB = 1024


class Prog:
    ENGINES = ("tensor", "vector", "scalar", "gpsimd", "sync")

    def __init__(self, nc, ctx):
        self.nc = nc
        self.q = {e: [] for e in self.ENGINES}
        self.sem = {}
        self.cnt = {}
        for name in ("dS", "dQ", "dW", "PE", "VE", "AC", "PO"):
            self.sem[name] = ctx.enter_context(nc.semaphore(name))
            self.cnt[name] = 0

    def emit(self, eng, fn, waits=(), inc=None, n=1):
        waits = tuple(w for w in waits if w is not None)
        tok = None
        if inc is not None:
            self.cnt[inc] += n
            tok = (inc, self.cnt[inc])
        self.q[eng].append((waits, fn, inc, n))
        return tok

    def dma(self, eng, out, in_, waits=(), sem="dS"):
        return self.emit(eng, lambda E, o=out, i=in_: E.dma_start(out=o, in_=i),
                         waits, sem, 16)

    def replay(self):
        with self.nc.Block() as block:
            for eng in self.ENGINES:
                queue = self.q[eng]
                sems = self.sem

                def body(E, queue=queue, sems=sems):
                    seen = {}
                    for waits, fn, inc, n in queue:
                        for s, v in waits:
                            if seen.get(s, 0) < v:
                                E.wait_ge(sems[s], v)
                                seen[s] = v
                        ins = fn(E)
                        if inc is not None:
                            ins.then_inc(sems[inc], n)

                getattr(block, eng)(body)


def _last(toks):
    best = {}
    for t in toks:
        if t is None:
            continue
        if isinstance(t, list):
            for tt in t:
                if tt is not None:
                    s, v = tt
                    if best.get(s, 0) < v:
                        best[s] = v
            continue
        s, v = t
        if best.get(s, 0) < v:
            best[s] = v
    return [(s, v) for s, v in best.items()]


def build_nc(debug=False):
    nc = bass.Bass("TRN2", target_bir_lowering=False)

    xfull = nc.dram_tensor("xfull", [S, D], F32, kind="ExternalInput").ap()
    xq = nc.dram_tensor("xq", [NTO * 128, D], F32, kind="ExternalInput").ap()
    wk = nc.dram_tensor("wk", [D, D], F32, kind="ExternalInput").ap()
    wvt = nc.dram_tensor("wvt", [D, D], F32, kind="ExternalInput").ap()
    wg = nc.dram_tensor("wg", [D, NEXP], F32, kind="ExternalInput").ap()
    bg = nc.dram_tensor("bg", [1, NEXP], F32, kind="ExternalInput").ap()
    wet = nc.dram_tensor("wet", [NEXP, D, D], F32, kind="ExternalInput").ap()
    bet = nc.dram_tensor("bet", [NEXP, D], F32, kind="ExternalInput").ap()
    masks = nc.dram_tensor("masks", [NTO, 128, MASKW], BF16, kind="ExternalInput").ap()
    ident = nc.dram_tensor("ident", [128, 128], F32, kind="ExternalInput").ap()
    out = nc.dram_tensor("out", [NTO * 128, D], F32, kind="ExternalOutput").ap()
    scratch_kind = "ExternalOutput" if debug else "Internal"
    xn_hbm = nc.dram_tensor("xn_hbm", [S, D], BF16, kind=scratch_kind).ap()
    g_hbm = nc.dram_tensor("g_hbm", [NTO * 128, D], BF16, kind=scratch_kind).ap()
    if debug:
        dbg_xqfm = nc.dram_tensor("dbg_xqfm", [128, NJ * NTO * 128], BF16,
                                  kind="ExternalOutput").ap()
        dbg_gatew = nc.dram_tensor("dbg_gatew", [128, NTO * NEXP], F32,
                                   kind="ExternalOutput").ap()
        dbg_qt = nc.dram_tensor("dbg_qt", [128, NTO * 1024], BF16,
                                kind="ExternalOutput").ap()
        dbg_z = nc.dram_tensor("dbg_z", [128, NTO * 1024], BF16,
                               kind="ExternalOutput").ap()
        dbg_st = nc.dram_tensor("dbg_st", [128, NTO * 1024], F32,
                                kind="ExternalOutput").ap()
        dbg_ast = nc.dram_tensor("dbg_ast", [128, NTO * 16], F32,
                                 kind="ExternalOutput").ap()

    with ExitStack() as ctx:
        E = ctx.enter_context
        P = Prog(nc, ctx)

        xn_fm = E(nc.sbuf_tensor("xn_fm", [128, NJ, S], BF16))
        xq_fm = E(nc.sbuf_tensor("xq_fm", [128, NJ, NTO * 128], BF16))
        wk_sb = E(nc.sbuf_tensor("wk_sb", [128, NJ, D], BF16))
        wvt_sb = E(nc.sbuf_tensor("wvt_sb", [128, NJ, D], BF16))
        idt = E(nc.sbuf_tensor("idt", [128, 128], BF16))
        wg_sb = E(nc.sbuf_tensor("wg_sb", [128, NJ, NEXP], BF16))
        bg_sb = E(nc.sbuf_tensor("bg_sb", [1, NEXP], BF16))
        be_sb = E(nc.sbuf_tensor("be_sb", [NEXP, D], BF16))
        ones1 = E(nc.sbuf_tensor("ones1", [1, 128], BF16))
        cst = E(nc.sbuf_tensor("cst", [128, 3], F32))
        stat_r = E(nc.sbuf_tensor("stat_r", [128, NTO], F32))
        stat_mr = E(nc.sbuf_tensor("stat_mr", [128, NTO], F32))
        mask_sb = E(nc.sbuf_tensor("mask_sb", [128, MASKW], BF16))
        lnstat_t = E(nc.sbuf_tensor("lnstat_t", [128, 16], F32))
        ast = E(nc.sbuf_tensor("ast", [128, 64], F32))

        ARENA_F32 = 19456
        arena = E(nc.sbuf_tensor("arena", [128, ARENA_F32], F32))
        a16 = arena.bitcast(BF16)

        def f32(off_kb, n):
            o = int(off_kb * KB)
            assert o % 4 == 0 and o + 4 * n <= ARENA_F32 * 4
            return arena[:, o // 4: o // 4 + n]

        def bf16(off_kb, n):
            o = int(off_kb * KB)
            assert o % 2 == 0 and o + 2 * n <= ARENA_F32 * 4
            return a16[:, o // 2: o // 2 + n]

        acc = f32(0, NTO * 512).rearrange("p (t n) -> p t n", t=NTO)
        xin = f32(32, 2 * D).rearrange("p (b n) -> p b n", b=2)
        xn_t = bf16(40, 4 * D).rearrange("p (b n) -> p b n", b=4)
        we_sb = bf16(48, 4 * 4 * 512).rearrange("p (b j n) -> p b j n", b=4, j=4)
        gate_w = f32(64, NTO * NEXP).rearrange("p (t e) -> p t e", t=NTO)
        gate_e = f32(64.5, NTO * NEXP).rearrange("p (t e) -> p t e", t=NTO)
        gmisc = f32(65, NTO * 16).rearrange("p (t c) -> p t c", t=NTO)
        wt_all = bf16(66, NTO * 128)
        g_stage = bf16(70, 2 * 512).rearrange("p (b n) -> p b n", b=2)
        lnstat = lnstat_t[:, :].rearrange("p (b c) -> p b c", b=2)
        scores = f32(0, SCW)
        p_sb = bf16(8, 4 * SCW).rearrange("p (b n) -> p b n", b=4)
        qbuf = bf16(24, D)
        qt_sb = bf16(26, NJ * 128)
        pt_sb = bf16(28, 4 * 256).rearrange("p (b n) -> p b n", b=4)
        z_acc = f32(30, D)
        z_sb = bf16(34, D)
        zt_sb = bf16(36, NJ * 128)
        kv_sb = bf16(38, KV_RING * D).rearrange("p (r n) -> p r n", r=KV_RING)
        xq_in = f32(54, 2 * D).rearrange("p (b n) -> p b n", b=2)
        g_in = bf16(62, 2 * D).rearrange("p (b n) -> p b n", b=2)
        out_sb = f32(66, D)
        xn_ep = f32(70, D)

        ps = [E(nc.psum_tensor(f"ps{i}", [128, 512], F32)) for i in range(8)]
        ps_ln = [ps[3].bitcast(BF16), ps[4].bitcast(BF16)]
        ps_qt = ps[2].bitcast(BF16)
        ps_wt = ps[2].bitcast(BF16)
        ps_pt = ps[7].bitcast(BF16)

        tok_id = P.dma("gpsimd", idt[:, :], ident[:, :], sem="dW")
        tok_wk = P.dma("gpsimd", wk_sb[:, :, :],
                       wk.rearrange("(c p) n -> p c n", p=128), sem="dW")
        tok_wvt = P.dma("gpsimd", wvt_sb[:, :, :],
                        wvt.rearrange("(c p) n -> p c n", p=128), sem="dW")
        tok_wg = P.dma("gpsimd", wg_sb[:, :, :],
                       wg.rearrange("(c p) e -> p c e", p=128), sem="dW")
        tok_bg = P.dma("gpsimd", bg_sb[:, :], bg[:, :], sem="dW")
        tok_be = P.dma("gpsimd", be_sb[:, :], bet[:, :], sem="dW")
        tok_ones = P.emit("gpsimd", lambda E: E.memset(ones1[:, :], 1.0), (), "PO")
        P.emit("gpsimd", lambda E: E.memset(cst[:, 0:1], 1.0 / D))
        P.emit("gpsimd", lambda E: E.memset(cst[:, 1:2], EPS))
        tok_cst = P.emit("gpsimd", lambda E: E.memset(cst[:, 2:3], -0.5), (), "PO")

        xin_free = [[], []]
        xnt_free = [[], [], [], []]
        lnstat_free = [None, None]
        psln_free = [None, None]
        ln_fm_tok = []
        xqfm_tok = [None] * NTO
        stat_tok = [None] * NTO
        gtile_ctr = [0]

        def ln_src(gi):
            if gi >= 32:
                return xq[(gi - 32) * 128:(gi - 31) * 128, :]
            return xfull[gi * 128:(gi + 1) * 128, :]

        sum_toks = {}

        def ln_load_sums(gi):
            par = gi % 2
            par4 = gi % 4
            d_tok = P.dma("sync", xin[:, par, :],
                          ln_src(gi), waits=_last(xin_free[par]), sem="dS")
            sum_toks[gi] = P.emit("vector", lambda E, a=lnstat[:, par, :],
                                  b=xin[:, par, :]:
                                  E.reduce_sum(a[:, 0:1], b, axis=AX, negate=True),
                                  waits=_last([d_tok, lnstat_free[par]]), inc="VE")

        def ln_stats_norm(gi):
            par = gi % 2
            par4 = gi % 4
            st = lnstat[:, par, :]
            xt = xin[:, par, :]
            t_nm = P.emit("scalar", lambda E, a=st:
                          E.mul(a[:, 1:2], a[:, 0:1], 1.0 / D),
                          waits=[sum_toks[gi]], inc="AC")
            t_ssq = P.emit("scalar", lambda E, a=st, b=xt, o=xn_t[:, par4, :]:
                           E.activation(out=o, in_=b, func=AF.Square,
                                        bias=a[:, 1:2], accum_out=a[:, 2:3]),
                           waits=_last([t_nm] + xnt_free[par4]), inc="AC")
            t_std = P.emit("scalar", lambda E, a=st:
                           E.activation(out=a[:, 3:4], in_=a[:, 2:3], func=AF.Sqrt,
                                        scale=cst[:, 0:1], bias=cst[:, 1:2]),
                           waits=[t_ssq, tok_cst], inc="AC")
            t_rstd = P.emit("vector", lambda E, a=st:
                            E.reciprocal(a[:, 4:5], a[:, 3:4]),
                            waits=[t_std], inc="VE")
            t_xn = P.emit("gpsimd", lambda E, a=st, b=xt, o=xn_t[:, par4, :]:
                          E.tensor_scalar(o, b, a[:, 1:2], a[:, 4:5],
                                          ALU.add, ALU.mult),
                          waits=_last([t_rstd, t_ssq]), inc="PO")
            lnstat_free[par] = t_xn
            if gi >= 32:
                sid = gi - 32
                t_sr = P.emit("vector", lambda E, a=st, sid=sid:
                              E.tensor_copy(stat_r[:, sid:sid + 1], a[:, 4:5]),
                              waits=[t_rstd], inc="VE")
                t_smr = P.emit("vector", lambda E, a=st, sid=sid:
                               E.tensor_tensor(stat_mr[:, sid:sid + 1],
                                               a[:, 1:2], a[:, 4:5], ALU.mult),
                               waits=[t_sr], inc="VE")
                stat_tok[sid] = t_smr
            xin_free[par] = [t_ssq, t_xn]
            return t_xn, t_ssq

        def ln_transpose_store(gi, t_xn, t_ssq):
            par4 = gi % 4
            own = gi >= 32
            bpar = gtile_ctr[0] % 2
            tr_tok = None
            for j in range(NJ):
                tr_tok = P.emit("tensor",
                                lambda E, j=j, b=bpar, src=xn_t[:, par4, :]:
                                E.transpose(ps_ln[b][:, j * 128:(j + 1) * 128],
                                            src[:, j * 128:(j + 1) * 128],
                                            idt[:, :]),
                                waits=_last([t_xn, tok_id, psln_free[bpar]]),
                                inc="PE" if j == NJ - 1 else None)
            if own:
                sid = gi - 32
                dest = xq_fm[:, :, sid * 128:(sid + 1) * 128]
            else:
                dest = xn_fm[:, :, gi * 128:(gi + 1) * 128]
            cp_eng = "vector" if (gtile_ctr[0] % 2 == 0) else "scalar"

            def _cp(E, d=dest, b=bpar):
                src = ps_ln[b][:, :].rearrange("p (j t) -> p j t", j=NJ)
                return (E.tensor_copy(d, src) if hasattr(E, "tensor_copy")
                        else E.copy(d, src))
            t_cp = P.emit(cp_eng, _cp, waits=[tr_tok],
                          inc="VE" if cp_eng == "vector" else "AC")
            psln_free[bpar] = t_cp
            if own:
                xqfm_tok[gi - 32] = t_cp
            else:
                ln_fm_tok.append(t_cp)
            gtile_ctr[0] += 1
            if not own:
                s_tok = P.dma("scalar",
                              xn_hbm[gi * 128:(gi + 1) * 128, :],
                              xn_t[:, par4, :], waits=[t_xn], sem="dQ")
                xnt_free[par4] = [s_tok, tr_tok]
            else:
                xnt_free[par4] = [tr_tok]

        pairs = [(h, e) for h in range(2) for e in range(NEXP)]
        tok_we = [None] * 4
        we_free = [None] * 4

        def emit_we_pair(i):
            h, e = pairs[i]
            for jh in range(2):
                buf = (2 * i + jh) % 4
                src = wet[e, jh * 512:(jh + 1) * 512,
                          h * 512:(h + 1) * 512].rearrange(
                    "(c p) n -> p c n", p=128)
                tok_we[buf] = P.dma("gpsimd", we_sb[:, buf, :, :], src,
                                    waits=_last([we_free[buf]]), sem="dW")

        emit_we_pair(0)
        emit_we_pair(1)

        ln_load_sums(32)
        for gi in range(32, 48):
            if gi + 1 < 48:
                ln_load_sums(gi + 1)
            t_xn, t_ssq = ln_stats_norm(gi)
            ln_transpose_store(gi, t_xn, t_ssq)

        gl = ps[2][:, 0:NTO * NEXP].rearrange("p (t e) -> p t e", t=NTO)
        mm_tok = [None] * NTO
        for t in range(NTO):
            for j in range(NJ):
                P.emit("tensor", lambda E, j=j, t=t:
                       E.matmul(gl[:, t, :],
                                xq_fm[:, j, t * 128:(t + 1) * 128],
                                wg_sb[:, j, :], start=(j == 0), stop=False),
                       waits=_last([xqfm_tok[t], tok_wg, tok_bg, tok_ones, tok_id]))
            mm_tok[t] = P.emit("tensor", lambda E, t=t:
                               E.matmul(gl[:, t, :], ones1[0:1, :], bg_sb[0:1, :],
                                        start=False, stop=True), inc="PE")
        tg = [None] * NTO
        for t in range(NTO):
            tg[t] = P.emit("vector", lambda E, t=t:
                           E.reduce_max(gmisc[:, t, 0:1], gl[:, t, :], axis=AX,
                                        negate=True),
                           waits=[mm_tok[NTO - 1]], inc="VE")
        te = [None] * NTO
        for t in range(NTO):
            te[t] = P.emit("scalar", lambda E, t=t:
                           E.activation(out=gate_e[:, t, :], in_=gl[:, t, :],
                                        func=AF.Exp, bias=gmisc[:, t, 0:1]),
                           waits=[tg[t]], inc="AC")
        t1 = [P.emit("vector", lambda E, t=t:
                     E.reduce_max(gmisc[:, t, 1:2], gate_e[:, t, :], axis=AX),
                     waits=[te[t]], inc="VE") for t in range(NTO)]
        t2 = [P.emit("vector", lambda E, t=t:
                     E.tensor_scalar(gmisc[:, t, 8:16], gate_e[:, t, :],
                                     gmisc[:, t, 1:2], 0.0, ALU.is_ge, ALU.add),
                     waits=[t1[t]], inc="VE") for t in range(NTO)]
        t3 = [P.emit("vector", lambda E, t=t:
                     E.scalar_tensor_tensor(gmisc[:, t, 8:16], gmisc[:, t, 8:16],
                                            -1e4, gate_e[:, t, :],
                                            ALU.mult, ALU.add),
                     waits=[t2[t]], inc="VE") for t in range(NTO)]
        t4 = [P.emit("vector", lambda E, t=t:
                     E.reduce_max(gmisc[:, t, 2:3], gmisc[:, t, 8:16], axis=AX),
                     waits=[t3[t]], inc="VE") for t in range(NTO)]
        t5 = [P.emit("vector", lambda E, t=t:
                     E.tensor_tensor(gmisc[:, t, 3:4], gmisc[:, t, 1:2],
                                     gmisc[:, t, 2:3], ALU.add),
                     waits=[t4[t]], inc="VE") for t in range(NTO)]
        t6 = [P.emit("vector", lambda E, t=t:
                     E.reciprocal(gmisc[:, t, 4:5], gmisc[:, t, 3:4]),
                     waits=[t5[t]], inc="VE") for t in range(NTO)]
        t7 = [P.emit("vector", lambda E, t=t:
                     E.tensor_scalar(gmisc[:, t, 8:16], gate_e[:, t, :],
                                     gmisc[:, t, 2:3], 0.0, ALU.is_ge, ALU.add),
                     waits=[t4[t]], inc="VE") for t in range(NTO)]
        t8 = [P.emit("vector", lambda E, t=t:
                     E.tensor_tensor(gmisc[:, t, 8:16], gmisc[:, t, 8:16],
                                     gate_e[:, t, :], ALU.mult),
                     waits=[t7[t]], inc="VE") for t in range(NTO)]
        gatew_tok = [None] * NTO
        for t in range(NTO):
            gatew_tok[t] = P.emit("vector", lambda E, t=t:
                                  E.tensor_scalar(gate_w[:, t, :],
                                                  gmisc[:, t, 8:16],
                                                  gmisc[:, t, 4:5],
                                                  0.0, ALU.mult, ALU.add),
                                  waits=[t8[t], t6[t]], inc="VE")
        wtall_tok = [None] * NTO
        p2_free = te[NTO - 1]

        def emit_wt_all():
            nonlocal p2_free
            tb16 = [P.emit("vector", lambda E, t=t:
                           E.tensor_copy(g_stage[:, 0, t * 8:(t + 1) * 8],
                                         gate_w[:, t, :]),
                           waits=[gatew_tok[t]], inc="VE") for t in range(NTO)]
            for t in range(NTO):
                t11 = P.emit("tensor", lambda E, t=t:
                             E.transpose(ps_wt[0:8, 256:384],
                                         g_stage[:, 0, t * 8:(t + 1) * 8],
                                         idt[:, :]),
                             waits=_last([tb16[t], p2_free] + te), inc="PE")
                t12 = P.emit("scalar", lambda E, t=t:
                             E.copy(wt_all[0:8, t * 128:(t + 1) * 128],
                                    ps_wt[0:8, 256:384]),
                             waits=[t11], inc="AC")
                wtall_tok[t] = t12
                p2_free = t12

        EXP_BANKS = [0, 1, 5, 6]
        psy_free = [None, None, None, None]
        acc_tok = [None] * NTO
        acc_free = [None] * NTO
        g_stage_free = [None, None]
        g_store_tok = []
        last_bias_mm = None

        ln_pending = []
        ln_load_sums(0)
        ln_load_sums(1)

        def ln_full_step(k):
            if k < 32:
                ln_pending.append((k, ln_stats_norm(k)))

        def ln_full_flush(upto):
            while ln_pending and ln_pending[0][0] <= upto:
                k, (t_xn, t_ssq) = ln_pending.pop(0)
                ln_transpose_store(k, t_xn, t_ssq)

        for i, (h, e) in enumerate(pairs):
            if i + 1 < len(pairs) and i >= 1:
                emit_we_pair(i + 1)
            if i == 1:
                emit_wt_all()
            ln_full_step(2 * i)
            ln_full_step(2 * i + 1)
            last_mm = None
            for t in range(NTO):
                if t == 8:
                    for k in (2 * i + 2, 2 * i + 3):
                        if k < 32:
                            ln_load_sums(k)
                    ln_full_flush(2 * i + 1)
                bpar = t % 4
                bank = EXP_BANKS[bpar]
                mm = None
                for j in range(NJ):
                    jh, jr = (0, j) if j < 4 else (1, j - 4)
                    buf = (2 * i + jh) % 4
                    mm = P.emit("tensor", lambda E, j=j, t=t, b=bank, bu=buf, jr=jr:
                                E.matmul(ps[b][:, :],
                                         xq_fm[:, j, t * 128:(t + 1) * 128],
                                         we_sb[:, bu, jr, :],
                                         start=(j == 0), stop=(j == NJ - 1)),
                                waits=_last([tok_we[buf], psy_free[bpar],
                                             xqfm_tok[t]]),
                                inc="PE" if j == NJ - 1 else None)
                last_mm = mm
                if e == 0:
                    tv = P.emit("vector", lambda E, t=t, b=bank:
                                E.tensor_scalar(acc[:, t, :], ps[b][:, :],
                                                gate_w[:, t, 0:1], 0.0,
                                                ALU.mult, ALU.add),
                                waits=_last([mm, gatew_tok[t], acc_free[t]]),
                                inc="VE")
                else:
                    tv = P.emit("vector", lambda E, t=t, b=bank, e=e:
                                E.scalar_tensor_tensor(acc[:, t, :], ps[b][:, :],
                                                       gate_w[:, t, e:e + 1],
                                                       acc[:, t, :],
                                                       ALU.mult, ALU.add),
                                waits=_last([mm, acc_tok[t]]), inc="VE")
                acc_tok[t] = tv
                psy_free[bpar] = tv
            we_free[(2 * i) % 4] = last_mm
            we_free[(2 * i + 1) % 4] = last_mm
            if e == NEXP - 1:
                for t in range(NTO):
                    mmb = P.emit("tensor", lambda E, t=t, h=h:
                                 E.matmul(ps[2][:, :],
                                          wt_all[0:8, t * 128:(t + 1) * 128],
                                          be_sb[0:8, h * 512:(h + 1) * 512],
                                          start=True, stop=True),
                                 waits=_last([wtall_tok[t], tok_be, p2_free]),
                                 inc="PE")
                    last_bias_mm = mmb
                    tb = P.emit("vector", lambda E, t=t:
                                E.tensor_tensor(acc[:, t, :], acc[:, t, :],
                                                ps[2][:, :], ALU.add),
                                waits=_last([mmb, acc_tok[t]]), inc="VE")
                    p2_free = tb
                    acc_tok[t] = tb
                    gpar = t % 2
                    tsil = P.emit("scalar", lambda E, t=t, b=gpar:
                                  E.activation(out=g_stage[:, b, :],
                                               in_=acc[:, t, :], func=AF.Silu),
                                  waits=_last([tb, g_stage_free[gpar]]), inc="AC")
                    gd = P.dma("gpsimd",
                               g_hbm[t * 128:(t + 1) * 128, h * 512:(h + 1) * 512],
                               g_stage[:, gpar, :], waits=[tsil], sem="dW")
                    g_stage_free[gpar] = gd
                    g_store_tok.append(gd)
                    acc_free[t] = tsil
        ln_full_flush(31)

        xn_stores_done = ("dQ", P.cnt["dQ"])
        ln_done = _last(ln_fm_tok + xqfm_tok + [psln_free[0], psln_free[1]])
        ln_xnt_done = _last(xnt_free[0] + xnt_free[1])

        if debug:
            P.dma("sync", dbg_xqfm[:, :].rearrange("p (j n) -> p j n", j=NJ),
                  xq_fm[:, :, :], waits=ln_done, sem="dS")

        moe_done = _last(acc_tok + g_store_tok + [p2_free, psy_free[0], psy_free[1],
                                                  last_bias_mm] + acc_free +
                         ln_done + ln_xnt_done + [xn_stores_done])

        if debug:
            P.dma("sync", dbg_gatew[:, :].rearrange("p (t e) -> p t e", t=NTO),
                  gate_w[:, :, :], waits=moe_done, sem="dS")

        def nchunks(s):
            return (256 * (s + 1) + SCW - 1) // SCW

        mask_free = [None]
        mask_tok = [None] * NTO
        xqin_free = [None, None]
        xqin_tok = [None] * NTO
        gin_free = [None, None]
        gin_tok = [None] * NTO
        out_free = [None]
        out_dma_toks = []

        kv_ring_free = [None] * KV_RING
        kv_cnt = [0]

        p01_free = [psy_free[0], psy_free[1]]
        p34_free = [psln_free[0], psln_free[1]]
        p56_free = [_last([psy_free[2], psy_free[3]])[0]]
        p2qt_free = [p2_free]
        p7_free = [None]
        ppt_free = [None, None]
        ptsb_free = [None] * 4
        qt_free = [None]
        qbuf_free = [None]

        sc_free = [None]
        p_free = [None] * 4
        pexp_tok = {}
        gsc = [0]
        slot_state = {}

        def prefetch_mask(s):
            barrier = list(moe_done) if s < 2 else []
            mask_tok[s] = P.dma("scalar", mask_sb[:, :], masks[s, :, :],
                                waits=_last([mask_free[0]] + barrier), sem="dQ")

        def prefetch_slot(s):
            par = s % 2
            barrier = list(moe_done) if s < 2 else []
            xqin_tok[s] = P.dma("gpsimd", xq_in[:, par, :],
                                xq[s * 128:(s + 1) * 128, :],
                                waits=_last([xqin_free[par]] + barrier), sem="dW")
            gin_tok[s] = P.dma("gpsimd", g_in[:, par, :],
                               g_hbm[s * 128:(s + 1) * 128, :],
                               waits=_last([gin_free[par], g_store_tok[s],
                                            g_store_tok[NTO + s]] + barrier),
                               sem="dW")

        def emit_A1(s):
            toks = []
            for half in range(2):
                mm = None
                for j in range(NJ):
                    mm = P.emit("tensor", lambda E, j=j, h=half, s=s:
                                E.matmul(ps[h][:, :],
                                         xq_fm[:, j, s * 128:(s + 1) * 128],
                                         wk_sb[:, j, h * 512:(h + 1) * 512],
                                         start=(j == 0), stop=(j == NJ - 1)),
                                waits=_last([tok_wk, xqfm_tok[s], p01_free[half]]),
                                inc="PE" if j == NJ - 1 else None)
                toks.append(mm)
            ev = []
            for half in range(2):
                e = P.emit("scalar", lambda E, h=half:
                           E.copy(qbuf[:, h * 512:(h + 1) * 512], ps[h][:, :]),
                           waits=_last([toks[half], qbuf_free[0]]), inc="AC")
                ev.append(e)
            p01_free[0] = ev[0]
            p01_free[1] = ev[1]
            return ev

        def emit_A2(s, ev):
            trt = None
            for j in range(NJ):
                trt = P.emit("tensor", lambda E, j=j:
                             E.transpose(ps_qt[:, j * 128:(j + 1) * 128],
                                         qbuf[:, j * 128:(j + 1) * 128], idt[:, :]),
                             waits=_last(ev + [p2qt_free[0]]),
                             inc="PE" if j == NJ - 1 else None)
            qbuf_free[0] = trt
            cq = P.emit("vector", lambda E:
                        E.tensor_copy(qt_sb[:, :], ps_qt[:, :]),
                        waits=_last([trt, qt_free[0]]), inc="VE")
            p2qt_free[0] = cq
            if debug:
                P.dma("sync", dbg_qt[:, s * 1024:(s + 1) * 1024], qt_sb[:, :],
                      waits=[cq], sem="dS")
            return cq

        def emit_S(s, c, cq_tok):
            W = min(SCW, 256 * (s + 1) - c * SCW)
            nsub = (W + 511) // 512
            ab = (s % 4) * 16
            last_chunk = (c == nchunks(s) - 1)
            evs = []
            submax = []
            last_mm = None
            for w in range(nsub):
                n = min(512, W - w * 512)
                bank = 3 + (w % 2)
                mm = None
                for j in range(NJ):
                    mm = P.emit("tensor",
                                lambda E, j=j, w=w, n=n, bank=bank, s=s, c=c:
                                E.matmul(ps[bank][:, 0:n],
                                         qt_sb[:, j * 128:(j + 1) * 128],
                                         xn_fm[:, j, c * SCW + w * 512:
                                               c * SCW + w * 512 + n],
                                         start=(j == 0), stop=(j == NJ - 1)),
                                waits=_last([cq_tok, p34_free[bank - 3]] + ln_done),
                                inc="PE" if j == NJ - 1 else None)
                last_mm = mm
                evac = P.emit("scalar", lambda E, w=w, n=n, bank=bank:
                              E.copy(scores[:, w * 512:w * 512 + n],
                                     ps[bank][:, 0:n]),
                              waits=_last([mm, sc_free[0]]), inc="AC")
                p34_free[bank - 3] = evac
                evs.append(evac)
                mwait = [evac]
                if last_chunk and w == nsub - 1:
                    mtok = P.emit("vector", lambda E, W=W, s=s:
                                  E.tensor_tensor(scores[:, W - MASKW:W],
                                                  scores[:, W - MASKW:W],
                                                  mask_sb[:, :], ALU.add),
                                  waits=_last([evac, mask_tok[s], mask_free[0]]),
                                  inc="VE")
                    mask_free[0] = mtok
                    mwait = [mtok]
                sm_t = P.emit("vector", lambda E, w=w, n=n, ab=ab:
                              E.reduce_max(ast[:, ab + 9 + w:ab + 10 + w],
                                           scores[:, w * 512:w * 512 + n],
                                           axis=AX, negate=True),
                              waits=_last(mwait), inc="VE")
                submax.append(sm_t)
            return evs, submax, last_mm

        def emit_softmax(s, c, evs, submax):
            W = min(SCW, 256 * (s + 1) - c * SCW)
            nsub = (W + 511) // 512
            par4 = gsc[0] % 4
            ab = (s % 4) * 16
            cm = submax[0]
            if nsub == 1:
                cm = P.emit("vector", lambda E, ab=ab, c=c:
                            E.tensor_copy(ast[:, ab + c:ab + c + 1],
                                          ast[:, ab + 9:ab + 10]),
                            waits=[submax[0]], inc="VE")
            else:
                for w in range(1, nsub):
                    src0 = (9 if w == 1 else c)
                    cm = P.emit("vector", lambda E, ab=ab, c=c, w=w, src0=src0:
                                E.tensor_tensor(ast[:, ab + c:ab + c + 1],
                                                ast[:, ab + src0:ab + src0 + 1],
                                                ast[:, ab + 9 + w:ab + 10 + w],
                                                ALU.min),
                                waits=_last([submax[w], cm]), inc="VE")
            if c == 0:
                bias_tok, bias_col = cm, ast[:, ab + 0:ab + 1]
            else:
                nm_t = P.emit("vector", lambda E, ab=ab:
                              E.tensor_tensor(ast[:, ab + 4:ab + 5],
                                              ast[:, ab + 0:ab + 1],
                                              ast[:, ab + 1:ab + 2], ALU.min),
                              waits=[cm], inc="VE")
                slot_state[s]["nm_tok"] = nm_t
                d0 = P.emit("vector", lambda E, ab=ab:
                            E.tensor_tensor(ast[:, ab + 5:ab + 6],
                                            ast[:, ab + 4:ab + 5],
                                            ast[:, ab + 0:ab + 1], ALU.subtract),
                            waits=[nm_t], inc="VE")
                a0 = P.emit("scalar", lambda E, ab=ab:
                            E.activation(out=ast[:, ab + 6:ab + 7],
                                         in_=ast[:, ab + 5:ab + 6], func=AF.Exp),
                            waits=[d0], inc="AC")
                slot_state[s]["a0_tok"] = a0
                bias_tok, bias_col = nm_t, ast[:, ab + 4:ab + 5]
            ex = P.emit("scalar", lambda E, W=W, b=bias_col, c=c, ab=ab, par4=par4:
                        E.activation(out=p_sb[:, par4, 0:W], in_=scores[:, 0:W],
                                     func=AF.Exp, bias=b,
                                     accum_out=ast[:, ab + 2 + c:ab + 3 + c]),
                        waits=_last([bias_tok, p_free[par4]]), inc="AC")
            pexp_tok[(s, c)] = ex
            sc_free[0] = ex
            return ex

        kv_map = {}
        kv_primed = {}

        kv_q = [0]

        def kv_dma(s, c, kt0, n):
            pos = kv_cnt[0] % KV_RING
            n = min(n, KV_RING - pos)
            gkt0 = c * (SCW // 128) + kt0
            rows = xn_hbm[gkt0 * 128:(gkt0 + n) * 128, :]
            queue, qsem = [("sync", "dS"), ("scalar", "dQ"),
                           ("gpsimd", "dW")][kv_q[0] % 3]
            kv_q[0] += 1
            waits = _last([kv_ring_free[pos + i] for i in range(n)] +
                          [xn_stores_done] +
                          (list(moe_done) if kv_cnt[0] < KV_RING else []))
            tok = P.dma(queue, kv_sb[:, pos:pos + n, :],
                        rows.rearrange("(c p) n -> p c n", p=128),
                        waits=waits, sem=qsem)
            for i in range(n):
                kv_map[(s, c, kt0 + i)] = (tok, pos + i)
            kv_cnt[0] += n
            return n

        def kv_prime(s):
            nkt = min(2 * (s + 1), SCW // 128)
            loaded = 0
            while loaded < min(nkt, KV_RING):
                loaded += kv_dma(s, 0, loaded,
                                 min(KV_CHUNK, nkt - loaded, KV_RING - loaded))
            kv_primed[s] = loaded

        def emit_Z(s, c):
            W = min(SCW, 256 * (s + 1) - c * SCW)
            nkt = W // 128
            par4 = slot_state[s]["sc_par"][c]
            ex = pexp_tok[(s, c)]
            ngrp = (nkt + 1) // 2
            cp_tok = [None] * ngrp
            pt_banks = [ps_pt, ps_qt]

            def z_group(g):
                kt0 = g * 2
                n = min(2, nkt - kt0)
                last = None
                for i in range(n):
                    kt = kt0 + i
                    dtok, ring = kv_map[(s, c, kt)]
                    for half in range(2):
                        last = P.emit(
                            "tensor",
                            lambda E, i=i, h=half, r=ring, g=g, kt=kt, nkt=nkt:
                            E.matmul(ps[5 + h][:, :],
                                     pt_sb[:, g % 4, i * 128:(i + 1) * 128],
                                     kv_sb[:, r, h * 512:(h + 1) * 512],
                                     start=(kt == 0), stop=(kt == nkt - 1)),
                            waits=_last([cp_tok[g], dtok] +
                                        ([p56_free[0]] if kt == 0 else [])),
                            inc="PE" if half == 1 else None)
                    kv_ring_free[ring] = last
                return last

            if c == 0:
                kv_loaded = kv_primed.get(s, 0)
            else:
                kv_loaded = 0
            while kv_loaded < min(nkt, KV_RING):
                kv_loaded += kv_dma(s, c, kv_loaded,
                                    min(KV_CHUNK, nkt - kv_loaded,
                                        KV_RING - kv_loaded))
            last_z = None

            def tr_group(g):
                kt0 = g * 2
                n = min(2, nkt - kt0)
                bankt = pt_banks[g % 2]
                tr = None
                for i in range(n):
                    kt = kt0 + i
                    tr = P.emit("tensor", lambda E, i=i, kt=kt, par4=par4, bk=bankt:
                                E.transpose(bk[:, i * 128:(i + 1) * 128],
                                            p_sb[:, par4, kt * 128:(kt + 1) * 128],
                                            idt[:, :]),
                                waits=_last([ex, ppt_free[g % 2]]),
                                inc="PE" if i == n - 1 else None)
                cp = P.emit("vector", lambda E, g=g, n=n, bk=bankt:
                            E.tensor_copy(pt_sb[:, g % 4, 0:n * 128],
                                          bk[:, 0:n * 128]),
                            waits=_last([tr, ptsb_free[g % 4]]), inc="VE")
                cp_tok[g] = cp
                ppt_free[g % 2] = cp
                if g % 2 == 0:
                    p7_free[0] = cp

            for g in range(min(2, ngrp)):
                tr_group(g)
            for g in range(ngrp):
                if g + 2 < ngrp:
                    tr_group(g + 2)
                z = z_group(g)
                ptsb_free[g % 4] = z
                last_z = z
                if g >= 1 and kv_loaded < nkt:
                    kv_loaded += kv_dma(s, c, kv_loaded, min(2, nkt - kv_loaded))
            p_free[par4] = last_z
            return last_z

        def emit_zfin(s):
            st = slot_state[s]
            ab = (s % 4) * 16
            nch = nchunks(s)
            if nch == 1:
                rs = P.emit("vector", lambda E, ab=ab:
                            E.reciprocal(ast[:, ab + 7:ab + 8],
                                         ast[:, ab + 2:ab + 3]),
                            waits=[pexp_tok[(s, 0)]], inc="VE")
                zs = []
                for half in range(2):
                    zz = P.emit("vector", lambda E, h=half, ab=ab:
                                E.tensor_scalar(z_sb[:, h * 512:(h + 1) * 512],
                                                ps[5 + h][:, :],
                                                ast[:, ab + 7:ab + 8], 0.0,
                                                ALU.mult, ALU.add),
                                waits=_last([st["z_toks"][-1], rs,
                                             st["zsb_free"]]),
                                inc="VE")
                    zs.append(zz)
                z_ready = zs
            else:
                a0 = st["a0_tok"]
                zts = []
                for half in range(2):
                    zt = P.emit("vector", lambda E, h=half, ab=ab:
                                E.scalar_tensor_tensor(
                                    z_acc[:, h * 512:(h + 1) * 512],
                                    z_acc[:, h * 512:(h + 1) * 512],
                                    ast[:, ab + 6:ab + 7], ps[5 + h][:, :],
                                    ALU.mult, ALU.add),
                                waits=_last([a0, st["z_toks"][-1],
                                             st["zacc_tok"]]),
                                inc="VE")
                    zts.append(zt)
                sm = P.emit("vector", lambda E, ab=ab:
                            E.scalar_tensor_tensor(ast[:, ab + 8:ab + 9],
                                                   ast[:, ab + 2:ab + 3],
                                                   ast[:, ab + 6:ab + 7],
                                                   ast[:, ab + 3:ab + 4],
                                                   ALU.mult, ALU.add),
                            waits=_last([a0, pexp_tok[(s, 1)]]), inc="VE")
                rs = P.emit("vector", lambda E, ab=ab:
                            E.reciprocal(ast[:, ab + 7:ab + 8],
                                         ast[:, ab + 8:ab + 9]),
                            waits=[sm], inc="VE")
                zz = P.emit("vector", lambda E, ab=ab:
                            E.tensor_scalar(z_sb[:, :], z_acc[:, :],
                                            ast[:, ab + 7:ab + 8], 0.0,
                                            ALU.mult, ALU.add),
                            waits=_last(zts + [rs, st["zsb_free"]]), inc="VE")
                z_ready = [zz]
            st["z_ready"] = z_ready

        def emit_FIN_ZTAV(s):
            st = slot_state[s]
            ab = (s % 4) * 16
            z_ready = st["z_ready"]
            trt = None
            for j in range(NJ):
                trt = P.emit("tensor", lambda E, j=j:
                             E.transpose(ps_pt[:, j * 128:(j + 1) * 128],
                                         z_sb[:, j * 128:(j + 1) * 128], idt[:, :]),
                             waits=_last(z_ready + [p7_free[0]]),
                             inc="PE" if j == NJ - 1 else None)
            czt = P.emit("vector", lambda E:
                         E.tensor_copy(zt_sb[:, :], ps_pt[:, :]),
                         waits=_last([trt, st["zt_free"]]), inc="VE")
            p7_free[0] = czt
            av = []
            for half in range(2):
                mm = None
                for j in range(NJ):
                    mm = P.emit("tensor", lambda E, j=j, h=half:
                                E.matmul(ps[5 + h][:, :],
                                         zt_sb[:, j * 128:(j + 1) * 128],
                                         wvt_sb[:, j, h * 512:(h + 1) * 512],
                                         start=(j == 0), stop=(j == NJ - 1)),
                                waits=_last([czt, tok_wvt] + z_ready),
                                inc="PE" if j == NJ - 1 else None)
                av.append(mm)
            if debug:
                P.dma("sync", dbg_z[:, s * 1024:(s + 1) * 1024], z_sb[:, :],
                      waits=_last(z_ready), sem="dS")
                P.dma("sync", dbg_ast[:, s * 16:(s + 1) * 16],
                      ast[:, ab:ab + 16], waits=_last(z_ready), sem="dS")
            st["av_toks"] = av

        def emit_EP(s):
            st = slot_state[s]
            av = st["av_toks"]
            xe = P.emit("scalar", lambda E, s=s:
                        E.activation(out=xn_ep[:, :], in_=xq_in[:, s % 2, :],
                                     func=AF.Identity, scale=stat_r[:, s:s + 1],
                                     bias=stat_mr[:, s:s + 1]),
                        waits=_last([xqin_tok[s], stat_tok[s], st["xnep_free"]]),
                        inc="AC")
            sts = []
            for half in range(2):
                t = P.emit("vector", lambda E, h=half:
                           E.tensor_tensor(xn_ep[:, h * 512:(h + 1) * 512],
                                           ps[5 + h][:, :],
                                           xn_ep[:, h * 512:(h + 1) * 512], ALU.add),
                           waits=_last([av[h], xe]), inc="VE")
                sts.append(t)
            dbg_tok = None
            if debug:
                dbg_tok = P.dma("sync", dbg_st[:, s * 1024:(s + 1) * 1024],
                                xn_ep[:, :], waits=_last(sts), sem="dS")
            sg = P.emit("vector", lambda E, s=s:
                        E.tensor_tensor(xn_ep[:, :], xn_ep[:, :],
                                        g_in[:, s % 2, :], ALU.mult),
                        waits=_last(sts + [gin_tok[s], dbg_tok]), inc="VE")
            ob = P.emit("vector", lambda E, s=s:
                        E.tensor_tensor(out_sb[:, :], xn_ep[:, :],
                                        xq_in[:, s % 2, :], ALU.add),
                        waits=_last([sg, out_free[0]]), inc="VE")
            od = P.dma("gpsimd", out[s * 128:(s + 1) * 128, :], out_sb[:, :],
                       waits=[ob], sem="dW")
            out_free[0] = od
            xqin_free[s % 2] = ob
            gin_free[s % 2] = ob
            return sts, sg, ob, od

        fin_prev = {"zsb": None, "zt": None, "xnep": None}

        def emit_AS(s):
            prefetch_mask(s)
            prefetch_slot(s)
            slot_state[s] = {"sc_par": {}, "z_toks": [], "S_last_mm": None}
            ev = emit_A1(s)
            cq = emit_A2(s, ev)
            slot_state[s]["cq"] = cq
            emit_Ssoft(s)

        def emit_Ssoft(s):
            cq = slot_state[s]["cq"]
            for c in range(nchunks(s)):
                slot_state[s]["sc_par"][c] = gsc[0] % 4
                evs, submax, last_mm = emit_S(s, c, cq)
                slot_state[s]["S_last_mm"] = last_mm
                emit_softmax(s, c, evs, submax)
                gsc[0] += 1
            qt_free[0] = slot_state[s]["S_last_mm"]

        def emit_UB(s):
            st = slot_state[s]
            st["zsb_free"] = fin_prev["zsb"]
            st["zt_free"] = fin_prev["zt"]
            st["xnep_free"] = fin_prev["xnep"]
            ppt_free[0] = p7_free[0]
            ppt_free[1] = p2qt_free[0]
            nch = nchunks(s)
            for c in range(nch):
                z = emit_Z(s, c)
                st["z_toks"].append(z)
                if nch > 1 and c == 0:
                    zes = []
                    for half in range(2):
                        ze = P.emit("vector", lambda E, h=half:
                                    E.tensor_copy(z_acc[:, h * 512:(h + 1) * 512],
                                                  ps[5 + h][:, :]),
                                    waits=[z], inc="VE")
                        zes.append(ze)
                    st["zacc_tok"] = zes[-1]
                    p56_free[0] = zes[-1]
            emit_zfin(s)
            p2qt_free[0] = ppt_free[1]
            if s + 2 < NTO:
                slot_state[s + 2] = {"sc_par": {}, "z_toks": [],
                                     "S_last_mm": None}
                ev2 = emit_A1(s + 2)
            emit_FIN_ZTAV(s)
            if s + 2 < NTO:
                cq = emit_A2(s + 2, ev2)
                slot_state[s + 2]["cq"] = cq
                prefetch_mask(s + 2)
            sts, sg, ob, od = emit_EP(s)
            p56_free[0] = _last(sts)[0]
            if s + 2 < NTO:
                emit_Ssoft(s + 2)
                prefetch_slot(s + 2)
            if s + 1 < NTO:
                kv_prime(s + 1)
            out_dma_toks.append(od)
            fin_prev["zsb"] = [("PE", P.cnt["PE"])] + \
                ([("dS", P.cnt["dS"])] if debug else [])
            fin_prev["zt"] = ("PE", P.cnt["PE"])
            fin_prev["xnep"] = ob

        emit_AS(0)
        emit_AS(1)
        kv_prime(0)
        for s in range(NTO):
            emit_UB(s)

        P.emit("gpsimd", lambda E: E.nop(), waits=_last(out_dma_toks))
        P.emit("scalar", lambda E: E.nop(), waits=_last(out_dma_toks))
        P.emit("sync", lambda E: E.nop(), waits=_last(out_dma_toks))

        P.replay()
    return nc


_NC_CACHE = {}


def _get_nc(debug=False):
    if debug not in _NC_CACHE:
        _NC_CACHE[debug] = build_nc(debug)
    return _NC_CACHE[debug]


def kernel(L0_attn_x=None, L0_attn_W=None, L0_ffn_Wg=None, L0_ffn_bg=None,
           L0_ffn_We=None, L0_ffn_be=None, _debug=False):
    x = np.ascontiguousarray(np.asarray(L0_attn_x, dtype=np.float32))
    W = np.asarray(L0_attn_W, dtype=np.float32)
    Wg = np.ascontiguousarray(np.asarray(L0_ffn_Wg, dtype=np.float32))
    bgv = np.ascontiguousarray(
        np.asarray(L0_ffn_bg, dtype=np.float32)).reshape(1, NEXP)
    We = np.asarray(L0_ffn_We, dtype=np.float32)
    be = np.ascontiguousarray(np.asarray(L0_ffn_be, dtype=np.float32))

    wk_h = np.ascontiguousarray(W[0:D, :])
    wvt_h = np.ascontiguousarray(W[D:2 * D, :].T)
    wet_h = np.ascontiguousarray(np.transpose(We, (0, 2, 1)))
    ident_h = np.eye(128, dtype=np.float32)

    import ml_dtypes
    r = np.arange(128)[:, None]
    c = np.arange(MASKW)[None, :]
    mask_par = []
    for p in range(2):
        m = np.where(c <= r + 128 * p, 0.0, NEG).astype(ml_dtypes.bfloat16)
        mask_par.append(np.ascontiguousarray(
            np.broadcast_to(m, (NTO, 128, MASKW))))

    nc = _get_nc(_debug)
    in_maps = []
    for core in range(8):
        b, p = core // 2, core % 2
        xb = x[b]
        own = xb.reshape(NTF, 128, D)[[2 * s + p for s in range(NTO)]].reshape(
            NTO * 128, D)
        in_maps.append({
            "xfull": xb,
            "xq": np.ascontiguousarray(own),
            "wk": wk_h, "wvt": wvt_h, "wg": Wg, "bg": bgv,
            "wet": wet_h, "bet": be,
            "masks": mask_par[p], "ident": ident_h,
        })
    res = run_bass_kernel_spmd(nc, in_maps, core_ids=list(range(8)))
    outv = np.empty((B, S, D), dtype=np.float32)
    for core in range(8):
        b, p = core // 2, core % 2
        shard = res.results[core]["out"].reshape(NTO, 128, D)
        for s in range(NTO):
            outv[b, (2 * s + p) * 128:(2 * s + p + 1) * 128, :] = shard[s]
    if _debug:
        return outv, res
    return outv
